# version 27
# baseline (speedup 1.0000x reference)
import numpy as np
from contextlib import ExitStack
import concourse.bass as bass
import concourse.mybir as mybir

F32 = mybir.dt.float32
BF16 = mybir.dt.bfloat16
U8 = mybir.dt.uint8
AF = mybir.ActivationFunctionType
OP = mybir.AluOpType
AX = mybir.AxisListType

ENGINES = ("sync", "tensor", "vector", "scalar", "gpsimd")
DMACH = ("cx", "chin", "cw", "ch1", "cout")

M64 = 1.5 * 2.0 ** 52
M32 = 1.5 * 2.0 ** 23
LN2 = float(np.log(2.0))


class Cfg:
    def __init__(self, L=4096, T=256):
        self.L = L
        self.T = T
        self.NSUB = L // T
        self.DM = 1024
        self.E = 4096
        self.DI = 2048
        self.NST = 16
        self.RL = 64
        self.NDT = self.DI // 128
        self.NDM = self.DM // 128
        self.NE = self.E // 128
        self.TT = T // 128
        self.NG = 4


class Plan:
    def __init__(self):
        self.recs = {e: [] for e in ENGINES}
        self.cnt = {e: 0 for e in ENGINES}
        for ch in DMACH:
            self.cnt[ch] = 0

    def op(self, eng, fn, after=(), dma=False, ch=None):
        waits = []
        for tok in after:
            if tok is None:
                continue
            if isinstance(tok, list):
                for tk in tok:
                    if tk is not None:
                        waits.append(tk)
                continue
            waits.append(tok)
        if dma:
            assert ch in DMACH
            if self.cnt[ch] > 0:
                waits.append((ch, self.cnt[ch]))
            self.cnt[ch] += 16
            self.recs[eng].append((fn, waits, (ch, 16)))
            return (ch, self.cnt[ch])
        self.cnt[eng] += 1
        self.recs[eng].append((fn, waits, (eng, 1)))
        return (eng, self.cnt[eng])

    def tok(self, eng):
        return (eng, self.cnt[eng])

    def finalize(self):
        marks = {e: set() for e in ENGINES}
        for e in ENGINES:
            for fn, waits, (isem, inc) in self.recs[e]:
                for (e2, v2) in waits:
                    if e2 in marks:
                        marks[e2].add(v2)
        remap = {}
        for e in ENGINES:
            sorted_marks = sorted(marks[e])
            remap[e] = {v: i + 1 for i, v in enumerate(sorted_marks)}
        self._remap = remap
        self._marks = marks

    def replay_all(self, eng, engobj, sems):
        emitted = 0
        waited = {}
        for fn, waits, (isem, inc) in self.recs[eng]:
            for (e2, v2) in waits:
                if e2 in self._remap:
                    v2r = self._remap[e2][v2]
                else:
                    v2r = v2
                if e2 == eng and v2 > emitted:
                    raise RuntimeError(f"self-wait deadlock on {eng}: need {v2}, at {emitted}")
                if waited.get(e2, -1) < v2r:
                    engobj.wait_ge(sems[e2], v2r)
                    waited[e2] = v2r
            inst = fn(engobj)
            if isem != eng:
                inst.then_inc(sems[isem], inc)
            else:
                emitted += inc
                if emitted in self._marks[eng]:
                    inst.then_inc(sems[eng], 1)


def build(cfg, nc):
    c = cfg
    T, TT, NDT, NDM, NE, NST, NG = c.T, c.TT, c.NDT, c.NDM, c.NE, c.NST, c.NG

    dr = {}
    dr["x"] = nc.dram_tensor("x", [c.L, c.DM], F32, kind="ExternalInput").ap()
    for l in range(2):
        dr[f"wiT{l}"] = nc.dram_tensor(f"wiT{l}", [c.DM, c.E], F32, kind="ExternalInput").ap()
        dr[f"woT{l}"] = nc.dram_tensor(f"woT{l}", [c.DI, c.DM], F32, kind="ExternalInput").ap()
        dr[f"wxdt{l}"] = nc.dram_tensor(f"wxdt{l}", [c.DI, c.RL], F32, kind="ExternalInput").ap()
        dr[f"wxB{l}"] = nc.dram_tensor(f"wxB{l}", [c.DI, NST], F32, kind="ExternalInput").ap()
        dr[f"wxC{l}"] = nc.dram_tensor(f"wxC{l}", [c.DI, NST], F32, kind="ExternalInput").ap()
        dr[f"wdtT{l}"] = nc.dram_tensor(f"wdtT{l}", [c.RL, c.DI], F32, kind="ExternalInput").ap()
        dr[f"alog{l}"] = nc.dram_tensor(f"alog{l}", [128, NDT * NST], F32, kind="ExternalInput").ap()
        dr[f"convw{l}"] = nc.dram_tensor(f"convw{l}", [128, NDT * 4], F32, kind="ExternalInput").ap()
        dr[f"convb{l}"] = nc.dram_tensor(f"convb{l}", [128, NDT], F32, kind="ExternalInput").ap()
        dr[f"dvec{l}"] = nc.dram_tensor(f"dvec{l}", [128, NDT], F32, kind="ExternalInput").ap()
        dr[f"bdt{l}"] = nc.dram_tensor(f"bdt{l}", [128, NDT], F32, kind="ExternalInput").ap()
    dr["w8T"] = nc.dram_tensor("w8T", [c.DM, c.DM], U8, kind="ExternalInput").ap()
    dr["fbias"] = nc.dram_tensor("fbias", [1, c.DM], F32, kind="ExternalInput").ap()
    dr["wscale"] = nc.dram_tensor("wscale", [1, 1], F32, kind="ExternalInput").ap()
    dr["out"] = nc.dram_tensor("out", [c.L, c.DM], F32, kind="ExternalOutput").ap()
    dr["h1s"] = nc.dram_tensor("h1s", [c.DM, c.L], BF16, kind="ExternalOutput").ap()

    S = ExitStack()
    sb = lambda n, sh, d=F32: S.enter_context(nc.sbuf_tensor(n, sh, d))
    psb = lambda n, sh, d=F32: S.enter_context(nc.psum_tensor(n, sh, d))

    t = {}
    t["wiT"] = sb("wiT", [128, NDM * c.E], BF16)
    t["woT"] = sb("woT", [128, NDT * c.DM], BF16)
    t["wxdt"] = sb("wxdt", [128, NDT * c.RL])
    t["wxB"] = sb("wxB", [128, NDT * NST])
    t["wxC"] = sb("wxC", [128, NDT * NST])
    t["wdtT"] = sb("wdtT", [64, NDT * 128], BF16)
    t["negA"] = sb("negA", [128, NDT * NST])
    t["convw"] = sb("convw", [128, NDT * 4])
    t["convb"] = sb("convb", [128, NDT])
    t["dvec"] = sb("dvec", [128, NDT])
    t["pbdt"] = sb("pbdt", [128, NDT])
    t["w8bT"] = sb("w8bT", [128, NDM * c.DM], mybir.dt.float8e4)
    t["u8s"] = sb("u8s", [128, c.DM // 2], U8)
    t["brep"] = sb("brep", [128, c.DM])
    t["wsrep"] = sb("wsrep", [128, 1])
    t["xn"] = sb("xn", [128, TT * c.DM])
    t["hin"] = sb("hin", [128, NDM * T], BF16)
    t["u"] = sb("ubuf", [128, NDT * T])
    t["gz"] = sb("gz", [128, NDT * T])
    t["crep"] = sb("crep", [128, NST * T], BF16)

    t["tails"] = sb("tails", [128, NDT * 3])
    t["cacc"] = sb("cacc", [128, T])
    t["dtlow"] = sb("dtlow", [64, T], BF16)
    t["sbCb"] = sb("sbCb", [16, T], BF16)
    t["sbBb"] = sb("sbBb", [16, T], BF16)
    t["mdtB"] = sb("mdtB", [128, T])
    t["vneg"] = sb("vneg", [128, T])
    t["slab"] = sb("slab", [128, TT * NG * 128])
    t["dA"] = sb("dA", [128, 2 * T])
    t["hsc"] = sb("hsc", [128, 2 * T])
    t["hC"] = sb("hCs", [128, T])
    t["yacc"] = sb("yacc", [128, T])
    t["yg"] = sb("ygb", [128, T], BF16)
    t["state"] = sb("state", [128, NDT * NST])
    t["idf"] = sb("idf", [128, 128])
    t["idb"] = sb("idb", [128, 128], BF16)
    t["dlt"] = sb("dlt", [16, NST * 128], mybir.dt.float8e4)
    t["flag"] = sb("flag", [128, 1])
    t["stat"] = sb("stat", [128, 16])
    t["qxT"] = sb("qxT", [128, NDM * T], BF16)
    t["gsml"] = sb("gsml", [128, c.DM])
    t["q1"] = sb("q1", [128, c.DM])
    t["q2"] = sb("q2", [128, c.DM])

    t["xs"] = sb("xs", [128, TT])
    t["wsxs"] = sb("wsxs", [128, TT])
    t["h1st2"] = sb("h1st2", [128, T], BF16)
    t["aa"] = t["cacc"].ap()[:, 0:T]
    t["ll"] = t["q1"].ap()[:, 0:T]
    t["rr"] = t["q1"].ap()[:, T:2 * T]
    t["mdtB1"] = t["q1"].ap()[:, 2 * T:3 * T]
    t["hscB"] = t["q2"].ap()[:, 0:2 * T]
    t["dA2"] = t["q2"].ap()[:, 2 * T:4 * T]
    t["sbB"] = t["mdtB"].ap()[0:16, 0:T]
    t["h2"] = t["vneg"].ap()
    t["h1st"] = t["yg"].ap()
    t["qxs2"] = t["yg"].ap()[:, 128:256]
    t["q3"] = t["gsml"].ap()[:, 0:c.DM]
    t["fsc2"] = t["slab"].ap()[:, 0:c.DM]
    t["ep1"] = t["q1"].ap()
    t["fsc"] = t["hsc"].ap()[:, 0:512]
    t["x"] = t["q2"]
    t["uraw"] = t["dA"].ap()[:, 0:T + 3]
    t["qxs"] = t["yg"].ap()[:, 0:128]

    pbank = [psb(f"pb{i}", [128, 512]) for i in range(8)]

    sems = {e: S.enter_context(nc.semaphore("sem_" + e)) for e in ENGINES}
    for ch in DMACH:
        sems[ch] = S.enter_context(nc.semaphore("sem_" + ch))

    pl = Plan()

    def dma(dst, src, after=(), ch="cw"):
        return pl.op("sync", lambda g, dst=dst, src=src: g.dma_start(out=dst, in_=src), after=after, dma=True, ch=ch)

    def act(fn, after=()):
        return pl.op("scalar", fn, after=after)

    def dve(fn, after=()):
        return pl.op("vector", fn, after=after)

    def pe(fn, after=()):
        return pl.op("tensor", fn, after=after)

    def gp(fn, after=()):
        return pl.op("gpsimd", fn, after=after)

    tk_iota = gp(lambda g: g.iota(t["idf"][:], [[1, 128]], base=0, channel_multiplier=-1,
                                  allow_small_or_imprecise_dtypes=True))
    tk_idf = gp(lambda g: g.tensor_scalar(t["idf"][:], t["idf"][:], 0.0, None, OP.is_equal), after=[tk_iota])
    tk_idb = dve(lambda g: g.tensor_copy(t["idb"][:], t["idf"][:]), after=[tk_idf])
    tk_rowg = gp(lambda g: g.iota(t["dlt"].ap().rearrange("j (n p) -> j n p", n=NST),
                                  [[0, NST], [0, 128]], base=0, channel_multiplier=1,
                                  allow_small_or_imprecise_dtypes=True), after=[tk_idf])
    tk_dlt = None
    for n in range(NST):
        tk_dlt = gp(lambda g, n=n: g.tensor_scalar(t["dlt"][:, n * 128:(n + 1) * 128],
                                                   t["dlt"][:, n * 128:(n + 1) * 128],
                                                   float(n), None, OP.is_equal), after=[tk_rowg])
    tk_f0 = dve(lambda g: g.memset(t["flag"][:], 1.0))
    tk_f1 = dve(lambda g: g.tensor_scalar(t["flag"][:], t["flag"][:], 2.0 ** -30, -1.0, OP.add, OP.add), after=[tk_f0])
    tk_flag = dve(lambda g: g.tensor_scalar(t["flag"][:], t["flag"][:], 0.0, None, OP.is_gt), after=[tk_f1])
    tk_state0 = dve(lambda g: g.memset(t["state"][:], 0.0))
    tk_cb1 = dve(lambda g: g.memset(t["stat"][:, 5:6], 1e-5))
    tk_cb2 = dve(lambda g: g.memset(t["stat"][:, 6:7], 1.0))
    tk_cb3 = dve(lambda g: g.memset(t["stat"][:, 7:8], 9.0 * LN2))
    tk_cb4 = dve(lambda g: g.memset(t["stat"][:, 13:14], 5.0 * LN2))
    tk_tails0 = dve(lambda g: g.memset(t["tails"][:], 0.0))

    def plan_rint(ap_get, after, pre_mul=None, pre_add=None, scratch="q3"):
        a_ = ap_get
        w = a_().shape[-1]
        sc = lambda: t[scratch][:, :w]
        deps = list(after)
        if pre_mul is not None:
            pm, pa = float(pre_mul), float(0.0 if pre_add is None else pre_add)
            t0 = dve(lambda g: g.tensor_scalar(a_(), a_(), pm, pa, OP.mult, OP.add), after=deps)
            deps = [t0]
        t1 = dve(lambda g: g.tensor_scalar(sc(), a_(), M32, -M32, OP.add, OP.add), after=deps)
        t2 = dve(lambda g: g.tensor_scalar(a_(), a_(), M64, -M64, OP.add, OP.add), after=deps + [t1])
        t3 = dve(lambda g: g.tensor_tensor(a_(), a_(), sc(), OP.subtract), after=[t1, t2])
        t4 = dve(lambda g: g.scalar_tensor_tensor(a_(), a_(), t["flag"][:], sc(), OP.mult, OP.add),
                 after=[t3, tk_flag])
        return t4

    def plan_layer_weights(l, after=()):
        toks = []
        stage = t["u"]
        prev = list(after)
        for dmt in range(NDM):
            tk_d = dma(stage[:, :c.E], dr[f"wiT{l}"][dmt * 128:(dmt + 1) * 128, :], after=prev)
            tk_c = act(lambda g, dmt=dmt: g.copy(t["wiT"][:, dmt * c.E:(dmt + 1) * c.E], stage[:, :c.E]), after=[tk_d])
            prev = [tk_c]
            toks.append(tk_c)
        for dt_i in range(NDT):
            tk_d = dma(stage[:, :c.DM], dr[f"woT{l}"][dt_i * 128:(dt_i + 1) * 128, :], after=prev)
            tk_c = act(lambda g, dt_i=dt_i: g.copy(t["woT"][:, dt_i * c.DM:(dt_i + 1) * c.DM], stage[:, :c.DM]), after=[tk_d])
            prev = [tk_c]
            toks.append(tk_c)
        toks.append(dma(t["wxdt"].ap().rearrange("p (n r) -> p n r", r=c.RL),
                        dr[f"wxdt{l}"].rearrange("(n p) r -> p n r", p=128), after=after))
        toks.append(dma(t["wxB"].ap().rearrange("p (n r) -> p n r", r=NST),
                        dr[f"wxB{l}"].rearrange("(n p) r -> p n r", p=128), after=after))
        toks.append(dma(t["wxC"].ap().rearrange("p (n r) -> p n r", r=NST),
                        dr[f"wxC{l}"].rearrange("(n p) r -> p n r", p=128), after=after))
        tk_wd = dma(stage[0:64, :c.DI], dr[f"wdtT{l}"], after=prev)
        tk_wdc = act(lambda g: g.copy(t["wdtT"][:], stage[0:64, :c.DI]), after=[tk_wd])
        prev = [tk_wdc]
        toks.append(tk_wdc)
        tk_al = dma(t["negA"][:], dr[f"alog{l}"], after=after)
        toks.append(act(lambda g: g.activation(t["negA"][:], t["negA"][:], AF.Exp), after=[tk_al]))
        toks.append(dma(t["convw"][:], dr[f"convw{l}"], after=after))
        toks.append(dma(t["convb"][:], dr[f"convb{l}"], after=after))
        toks.append(dma(t["dvec"][:], dr[f"dvec{l}"], after=after))
        toks.append(dma(t["pbdt"][:], dr[f"bdt{l}"], after=after))
        return toks

    def plan_ffn_weights(after=()):
        toks = []
        fb = t["xn"].ap()[:, 0:c.DM]
        fe = t["q2"].ap()[:, :c.DM]
        sgn = t["q1"].ap()[:, :c.DM]
        prev = list(after)
        for dmt in range(NDM):
            kd0 = dma(t["u8s"][:], dr["w8T"][dmt * 128:(dmt + 1) * 128, 0:c.DM // 2], after=prev)
            k10 = dve(lambda g: g.tensor_copy(fb[:, 0:c.DM // 2], t["u8s"][:]), after=[kd0] + prev)
            kd1 = dma(t["u8s"][:], dr["w8T"][dmt * 128:(dmt + 1) * 128, c.DM // 2:], after=[k10])
            k1 = dve(lambda g: g.tensor_copy(fb[:, c.DM // 2:], t["u8s"][:]), after=[kd1])
            k2 = dve(lambda g: g.tensor_scalar(fe, fb, 128.0, None, OP.is_ge), after=[k1])
            k3 = dve(lambda g: g.scalar_tensor_tensor(fb, fe, -128.0, fb, OP.mult, OP.add), after=[k2])
            k4 = dve(lambda g: g.tensor_scalar(sgn, fe, -2.0, 1.0, OP.mult, OP.add), after=[k3])
            k5 = dve(lambda g: g.tensor_copy(fe, fb), after=[k4])
            k6 = plan_rint(lambda: fe, [k5], pre_mul=0.125, pre_add=-0.4375)
            k7 = dve(lambda g: g.scalar_tensor_tensor(fb, fe, -8.0, fb, OP.mult, OP.add), after=[k6])
            k8 = dve(lambda g: g.tensor_scalar(fb, fb, 8.0, None, OP.add), after=[k7])
            k9 = dve(lambda g: g.tensor_scalar(t["q3"][:, :c.DM], fe, 0.0, None, OP.is_gt), after=[k6, k8])
            k10 = act(lambda g: g.activation(fe, fe, AF.Exp, scale=LN2), after=[k9])
            k11 = plan_rint(lambda: fe, [k10], scratch="fsc2")
            k12 = dve(lambda g: g.tensor_tensor(fe, fe, fb, OP.mult), after=[k11])
            k13 = dve(lambda g: g.tensor_scalar(fe, fe, 1.0 / 2048, None, OP.mult), after=[k12])
            k14 = dve(lambda g: g.tensor_scalar(fb, fb, -8.0, 1.0 / 1024, OP.add, OP.mult), after=[k13])
            k15 = dve(lambda g: g.tensor_tensor(fe, fe, fb, OP.subtract), after=[k14])
            k16 = dve(lambda g: g.tensor_tensor(fe, fe, t["q3"][:, :c.DM], OP.mult), after=[k15])
            k17 = dve(lambda g: g.tensor_tensor(fe, fe, fb, OP.add), after=[k16])
            k18 = dve(lambda g, dmt=dmt: g.tensor_tensor(t["w8bT"][:, dmt * c.DM:(dmt + 1) * c.DM],
                                                         fe, sgn, OP.mult), after=[k17])
            prev = [k18]
            toks.append(k18)
        tk_fb2 = dma(t["q1"][0:1, :c.DM], dr["fbias"], after=prev)
        tk_ws = dma(t["q1"][0:1, c.DM - 1:c.DM], dr["wscale"], after=[tk_fb2])
        tk_on = dve(lambda g: g.memset(t["vneg"][0:1, 0:128], 1.0), after=prev)
        ones = t["vneg"].ap()[0:1, 0:128]
        m1 = pe(lambda g, ones=ones: g.matmul(pbank[0][:, 0:512], ones, t["q1"][0:1, 0:512], start=True, stop=True),
                after=[tk_fb2, tk_ws, tk_on] + list(after))
        m2 = pe(lambda g, ones=ones: g.matmul(pbank[1][:, 0:512], ones, t["q1"][0:1, 512:1024], start=True, stop=True),
                after=[m1])
        toks.append(dve(lambda g: g.tensor_copy(t["brep"][:, 0:512], pbank[0][:, 0:512]), after=[m1]))
        toks.append(dve(lambda g: g.tensor_copy(t["brep"][:, 512:1024], pbank[1][:, 0:512]), after=[m2]))
        toks.append(dve(lambda g: g.tensor_scalar(t["wsrep"][:], t["brep"][:, c.DM - 1:c.DM], 2.0, None, OP.mult),
                        after=[toks[-1]]))
        tk_fb3 = dma(t["q1"][0:1, 0:1], dr["fbias"][0:1, c.DM - 1:c.DM], after=[tk_ws] + toks[-1:])
        m3 = pe(lambda g, ones=ones: g.matmul(pbank[2][:, 0:1], ones, t["q1"][0:1, 0:1], start=True, stop=True),
                after=[tk_fb3, m2])
        toks.append(dve(lambda g: g.tensor_copy(t["brep"][:, c.DM - 1:c.DM], pbank[2][:, 0:1]), after=[m3]))
        return toks

    def plan_subtile(l, s, fine):
        is_l0 = (l == 0)
        xoff = s * T
        sq = t["q1"]
        prev = fine["full"]
        f_x = fine.get("xfree", prev)
        f_o1 = fine.get("o1s", prev)
        f_hin = fine.get("hinfree", prev)
        f_b01 = fine.get("b01", prev)
        f_b23 = fine.get("b23", prev)
        f_b4567 = fine.get("b4567", prev)
        f_da = fine.get("dafree", prev)
        f_u = fine.get("ufree", prev)
        f_crep = fine.get("crepfree", prev)
        f_q = fine.get("qfree", prev)

        tk_x = []
        tk_hin = []
        if not is_l0:
            for dmt in range(NDM):
                tk_hin.append(dma(t["hin"][:, dmt * T:(dmt + 1) * T],
                                  dr["h1s"][dmt * 128:(dmt + 1) * 128, xoff:xoff + T],
                                  after=f_hin, ch="chin"))

        tk_xn = []
        for tt in range(TT):
            xtt = t["x"][:, 0:c.DM]
            tk_xld = dma(t["x"][:, 0:c.DM], dr["x"][xoff + tt * 128: xoff + (tt + 1) * 128, :],
                         after=f_x + f_q + (tk_xn[-1:] if tk_xn else []), ch="cx")
            tk_x.append(tk_xld)
            s0 = t["stat"][:, tt * 8 + 0: tt * 8 + 1]
            s1 = t["stat"][:, tt * 8 + 1: tt * 8 + 2]
            s2 = t["stat"][:, tt * 8 + 2: tt * 8 + 3]
            s3 = t["stat"][:, tt * 8 + 3: tt * 8 + 4]
            a1 = dve(lambda g, xtt=xtt, s0=s0: g.tensor_reduce(s0, xtt, AX.X, OP.add), after=[tk_x[tt]] + f_q)
            a2 = dve(lambda g, xtt=xtt: g.tensor_tensor(sq[:, :c.DM], xtt, xtt, OP.mult), after=[tk_x[tt]] + f_q)
            tk_x[tt] = tk_x[tt]
            a3 = dve(lambda g, s1=s1: g.tensor_reduce(s1, sq[:, :c.DM], AX.X, OP.add), after=[a2])
            a4 = dve(lambda g, s0=s0: g.tensor_scalar(s0, s0, 1.0 / c.DM, None, OP.mult), after=[a1])
            a5 = dve(lambda g, s1=s1: g.tensor_scalar(s1, s1, 1.0 / c.DM, None, OP.mult), after=[a3])
            a6 = dve(lambda g, s0=s0, s2=s2: g.tensor_tensor(s2, s0, s0, OP.mult), after=[a4])
            a7 = dve(lambda g, s1=s1, s2=s2: g.tensor_tensor(s2, s1, s2, OP.subtract), after=[a5, a6])
            a8 = act(lambda g, s2=s2, s3=s3: g.activation(s3, s2, AF.Ln, bias=t["stat"][:, 5:6]), after=[a7, tk_cb1])
            a9 = act(lambda g, s3=s3: g.activation(s3, s3, AF.Exp, scale=-0.5), after=[a8])
            a10 = dve(lambda g, s1=s1, s3=s3: g.tensor_tensor(s1, s3, s3, OP.mult), after=[a9])
            a11 = dve(lambda g, s2=s2: g.tensor_scalar(s2, s2, 1e-5, None, OP.add), after=[a10, a7])
            a12 = dve(lambda g, s1=s1, s2=s2: g.tensor_tensor(s1, s1, s2, OP.mult), after=[a10, a11])
            a13 = dve(lambda g, s1=s1: g.tensor_scalar(s1, s1, -0.5, 1.5, OP.mult, OP.add), after=[a12])
            a14 = dve(lambda g, s1=s1, s3=s3: g.tensor_tensor(s3, s3, s1, OP.mult), after=[a13])
            a15 = dve(lambda g, s0=s0, s3=s3: g.tensor_tensor(s0, s0, s3, OP.mult), after=[a14, a4])
            a16 = dve(lambda g, s0=s0: g.tensor_scalar(s0, s0, -1.0, None, OP.mult), after=[a15])
            a17 = act(lambda g, xtt=xtt, tt=tt, s0=s0, s3=s3:
                      g.activation(t["xn"][:, tt * c.DM:(tt + 1) * c.DM], xtt, AF.Identity, scale=s3, bias=s0),
                      after=[a16] + f_o1)
            tk_xn.append(a17)

        tk_wsxs = [None, None]
        tk_q2done = [None, None]
        if not is_l0:
            qprev = list(f_q)
            for tt in range(TT):
                xn_tt = t["xn"][:, tt * c.DM:(tt + 1) * c.DM]
                xs_c = t["xs"][:, tt:tt + 1]
                q1 = t["q1"].ap()[:, :c.DM]
                q2 = t["q2"].ap()[:, :c.DM]
                rxs = t["stat"][:, tt * 8 + 4: tt * 8 + 5]
                b1 = dve(lambda g, xn_tt=xn_tt, xs_c=xs_c: g.tensor_reduce(xs_c, xn_tt, AX.X, OP.max,
                                                                           apply_absolute_value=True),
                         after=[tk_xn[tt], tk_xn[1]] + qprev)
                b2 = dve(lambda g, xs_c=xs_c: g.tensor_scalar(xs_c, xs_c, 1e-12, 1.0 / 448.0, OP.max, OP.mult),
                         after=[b1])
                b3 = dve(lambda g, xs_c=xs_c, tt=tt: g.tensor_tensor(t["wsxs"][:, tt:tt + 1], xs_c, t["wsrep"][:],
                                                                     OP.mult), after=[b2])
                tk_wsxs[tt] = b3
                b4 = dve(lambda g, xs_c=xs_c, rxs=rxs: g.reciprocal(rxs, xs_c), after=[b2])
                b5 = dve(lambda g, xn_tt=xn_tt, rxs=rxs: g.tensor_scalar(q1, xn_tt, rxs, None, OP.mult),
                         after=[b4] + qprev)
                b6 = dve(lambda g: g.tensor_tensor(q2, q1, q1, OP.mult), after=[b5] + qprev)
                b7 = dve(lambda g: g.tensor_scalar(q2, q2, 1e-30, None, OP.max), after=[b6])
                b8 = act(lambda g: g.activation(q2, q2, AF.Ln), after=[b7])
                b9 = plan_rint(lambda: q2, [b8], pre_mul=0.5 / LN2, pre_add=-0.5)
                b10 = dve(lambda g: g.tensor_scalar(q2, q2, -6.0, -3.0, OP.max, OP.add), after=[b9])
                b11 = act(lambda g: g.activation(t["fsc2"][:, :c.DM], q2, AF.Exp, scale=LN2, bias=t["stat"][:, 7:8]),
                          after=[b10, tk_cb3] + qprev)
                b12 = plan_rint(lambda: t["fsc2"][:, :c.DM], [b11])
                b13 = act(lambda g: g.activation(q2, q2, AF.Exp, scale=-LN2, bias=t["stat"][:, 13:14]), after=[b12, b10, tk_cb4])
                b14 = plan_rint(lambda: q2, [b13])
                b15 = dve(lambda g: g.tensor_tensor(q2, q1, q2, OP.mult), after=[b14])
                b16 = plan_rint(lambda: q2, [b15], pre_mul=2.0 ** -5)
                b17 = dve(lambda g: g.tensor_tensor(q2, q2, t["fsc2"][:, :c.DM], OP.mult), after=[b16])
                if tt == 0:
                    m1_ = dve(lambda g: g.tensor_copy(t["hsc"][:], q2[:, 0:512]), after=[b17])
                    m2_ = dve(lambda g: g.tensor_copy(t["vneg"][:], q2[:, 512:768]), after=[b17])
                    m3_ = dve(lambda g: g.tensor_copy(t["yacc"][:], q2[:, 768:1024]), after=[b17, m2_])
                    tk_q2done[tt] = m3_
                    qprev = [m1_, m3_]
                else:
                    tk_q2done[tt] = b17
                    qprev = [b17]

        if is_l0:
            tk_bank_last = [None, None]
            for tt in range(TT):
                for dmt in range(NDM):
                    bnk = dmt % 2
                    reg = pbank[6 + bnk][:, ((dmt // 2) % 2) * 128:((dmt // 2) % 2) * 128 + 128]
                    tk_t = pe(lambda g, tt=tt, dmt=dmt, reg=reg:
                              g.transpose(reg, t["xn"][:, tt * c.DM + dmt * 128: tt * c.DM + (dmt + 1) * 128],
                                          t["idf"][:]),
                              after=[tk_xn[tt], tk_idf, tk_bank_last[bnk]] + f_b4567)
                    tk_e = act(lambda g, tt=tt, dmt=dmt, reg=reg:
                               g.copy(t["hin"][:, dmt * T + tt * 128: dmt * T + (tt + 1) * 128], reg), after=[tk_t])
                    tk_bank_last[bnk] = tk_e
                    tk_hin.append(tk_e)

        tk_u = [None] * NDT
        tk_gz = [None] * NDT
        bank_free = [None] * 6
        tk_xz_last = None
        conv_prev = []
        for e in range(NE):
            bk = (e + 2) % 6
            ps = pbank[bk][:, 0:T]
            mm = None
            for dmt in range(NDM):
                mm = pe(lambda g, e=e, dmt=dmt, ps=ps:
                        g.matmul(ps, t["wiT"][:, dmt * c.E + e * 128: dmt * c.E + (e + 1) * 128],
                                 t["hin"][:, dmt * T:(dmt + 1) * T],
                                 start=(dmt == 0), stop=(dmt == NDM - 1)),
                        after=(tk_hin + [bank_free[bk]] +
                               (f_b01 if bk < 2 else (f_b23 if bk < 4 else f_b4567))) if dmt == 0 else [])
                if dmt == NDM - 1:
                    tk_xz_last = mm
            if e < NDT:
                d = e
                c0 = dve(lambda g, d=d: g.tensor_copy(t["uraw"][:, 0:3], t["tails"][:, d * 3:(d + 1) * 3]),
                         after=[mm] + conv_prev + f_da + f_u)
                c1 = act(lambda g, ps=ps: g.copy(t["uraw"][:, 3:3 + T], ps), after=[mm, c0])
                bank_free[bk] = c1
                c2 = dve(lambda g, d=d: g.tensor_copy(t["tails"][:, d * 3:(d + 1) * 3], t["uraw"][:, T:T + 3]),
                         after=[c1])
                ca = t["cacc"].ap()[:, 0:T]
                c3 = dve(lambda g, d=d, ca=ca: g.tensor_scalar(ca, t["uraw"][:, 0:T],
                                                               t["convw"][:, d * 4:d * 4 + 1],
                                                               t["convb"][:, d:d + 1], OP.mult, OP.add), after=[c1])
                c4 = dve(lambda g, d=d, ca=ca:
                         g.scalar_tensor_tensor(ca, t["uraw"][:, 1:1 + T], t["convw"][:, d * 4 + 1:d * 4 + 2], ca,
                                                OP.mult, OP.add), after=[c3])
                c5 = dve(lambda g, d=d, ca=ca:
                         g.scalar_tensor_tensor(ca, t["uraw"][:, 2:2 + T], t["convw"][:, d * 4 + 2:d * 4 + 3], ca,
                                                OP.mult, OP.add), after=[c4])
                c6 = dve(lambda g, d=d, ca=ca:
                         g.scalar_tensor_tensor(ca, t["uraw"][:, 3:3 + T], t["convw"][:, d * 4 + 3:d * 4 + 4], ca,
                                                OP.mult, OP.add), after=[c5])
                c7 = act(lambda g, d=d, ca=ca: g.activation(t["u"][:, d * T:(d + 1) * T], ca, AF.Silu),
                         after=[c6] + f_u)
                conv_prev = [c2, c7]
                tk_u[d] = c7
            else:
                d = e - NDT
                c1 = act(lambda g, ps=ps, d=d: g.activation(t["gz"][:, d * T:(d + 1) * T], ps, AF.Silu),
                         after=[mm] + f_u)
                bank_free[bk] = c1
                tk_gz[d] = c1

        mmd = mmb = mmc = None
        for dt_i in range(NDT):
            mmd = pe(lambda g, dt_i=dt_i: g.matmul(pbank[7][0:64, 0:T],
                                                   t["wxdt"][:, dt_i * c.RL:(dt_i + 1) * c.RL],
                                                   t["u"][:, dt_i * T:(dt_i + 1) * T],
                                                   start=(dt_i == 0), stop=(dt_i == NDT - 1)),
                     after=(tk_u + f_b4567) if dt_i == 0 else [])
        for dt_i in range(NDT):
            mmb = pe(lambda g, dt_i=dt_i: g.matmul(pbank[6][0:16, 0:T],
                                                   t["wxB"][:, dt_i * NST:(dt_i + 1) * NST],
                                                   t["u"][:, dt_i * T:(dt_i + 1) * T],
                                                   start=(dt_i == 0), stop=(dt_i == NDT - 1)),
                     after=([tk_hin[-1]] if is_l0 else []) if dt_i == 0 else [])
        for dt_i in range(NDT):
            mmc = pe(lambda g, dt_i=dt_i: g.matmul(pbank[4][0:16, 0:T],
                                                   t["wxC"][:, dt_i * NST:(dt_i + 1) * NST],
                                                   t["u"][:, dt_i * T:(dt_i + 1) * T],
                                                   start=(dt_i == 0), stop=(dt_i == NDT - 1)),
                     after=[bank_free[4]] if dt_i == 0 else [])
        ev_dt = act(lambda g: g.copy(t["dtlow"][:], pbank[7][0:64, 0:T]), after=[mmd] + f_da)
        ev_b = act(lambda g: g.activation(t["sbBb"][:], pbank[6][0:16, 0:T], AF.Copy, scale=-1.0),
                   after=[mmb] + f_da)
        ev_c = act(lambda g: g.copy(t["sbCb"][:], pbank[4][0:16, 0:T]), after=[mmc] + f_da)
        tk_bt = [ev_b]
        btwar = ev_b
        tk_crep = []
        for n in range(NST):
            bk, half = n // 2, n % 2
            reg = pbank[bk][:, half * T: half * T + T]
            mmn = pe(lambda g, n=n, reg=reg: g.matmul(reg, t["dlt"][:, n * 128:(n + 1) * 128], t["sbCb"][:],
                                                      start=True, stop=True),
                     after=[ev_c, tk_dlt, tk_bt[-1] if tk_bt else None] +
                           ([bank_free[bk]] if bk < 6 else []) +
                           ([ev_dt] if bk == 7 else []) + ([ev_b] if bk == 6 else []))
            evn = act(lambda g, n=n, reg=reg: g.copy(t["crep"][:, n * T:(n + 1) * T], reg),
                      after=[mmn] + (f_crep if n == 0 else []))
            tk_crep.append(evn)

        tk_qxT = []
        if not is_l0:
            tk_cst = []
            tk_tpq = []
            for tt in range(TT):
                for dmt in range(NDM):
                    k = tt * NDM + dmt
                    qxsb = t["qxs"] if k % 2 == 0 else t["qxs2"]
                    regb = pbank[4 + (k % 2)].ap().bitcast(BF16)[:, 512:640]
                    if tt == 0:
                        if dmt < 4:
                            qsrc = t["hsc"].ap()[:, dmt * 128:(dmt + 1) * 128]
                        elif dmt < 6:
                            qsrc = t["vneg"].ap()[:, (dmt - 4) * 128:(dmt - 3) * 128]
                        else:
                            qsrc = t["yacc"].ap()[:, (dmt - 6) * 128:(dmt - 5) * 128]
                    else:
                        qsrc = t["q2"].ap()[:, dmt * 128:(dmt + 1) * 128]
                    cst = dve(lambda g, qsrc=qsrc, qxsb=qxsb:
                              g.tensor_scalar(qxsb[:], qsrc, 2.0 ** -9, None, OP.mult),
                              after=[tk_q2done[tt], (tk_tpq[-2] if k >= 2 else None)] +
                                    (f_hin if k < 2 else []))
                    tpq = pe(lambda g, regb=regb, qxsb=qxsb: g.transpose(regb, qxsb[:], t["idb"][:]),
                             after=[cst, tk_idb, (tk_qxT[-2] if k >= 2 else None)] +
                                   ([tk_crep[-1]] + f_o1 if k < 2 else []))
                    evq = act(lambda g, dmt=dmt, tt=tt, regb=regb:
                              g.copy(t["qxT"][:, dmt * T + tt * 128: dmt * T + (tt + 1) * 128], regb),
                              after=[tpq])
                    tk_cst.append(cst)
                    tk_tpq.append(tpq)
                    tk_qxT.append(evq)

        tk_pe_phaseA = pl.tok("tensor")

        mm4_first = set()
        tk_dt_war = []
        _q = tk_qxT[-1] if tk_qxT else None
        tk_dA_war = [_q, _q, _q, _q]
        tk_h_war = [[_q] if _q else None, [_q] if _q else None]
        tk_mdt_readers = [[], []]
        tk_sp_war = None
        tk_brep_war = [[], []]
        tk_dbu_war = [None, None]
        tk_vneg_war = None
        tk_hC_war = [None, None]
        tk_yaccA_war = None
        tk_yg_war = None
        tk_mm4_last = None
        sg3p = [None, None]

        def emit_pchain(dt_i, extra):
            mmq = pe(lambda g, dt_i=dt_i: g.matmul(pbank[4].ap()[:, 0:T], t["wdtT"][:, dt_i * 128:(dt_i + 1) * 128],
                                                   t["dtlow"][:], start=True, stop=True),
                     after=[ev_dt] + extra)
            p1 = act(lambda g, dt_i=dt_i: g.activation(t["aa"][:], pbank[4].ap()[:, 0:T], AF.Abs,
                                                       bias=t["pbdt"][:, dt_i:dt_i + 1]),
                     after=[mmq, tk_sp_war])
            p2 = act(lambda g: g.activation(t["aa"][:], t["aa"][:], AF.Exp, scale=-1.0), after=[p1])
            p3 = act(lambda g: g.activation(t["ll"][:], t["aa"][:], AF.Ln, bias=t["stat"][:, 6:7]), after=[p2, tk_cb2])
            p4 = act(lambda g, dt_i=dt_i: g.activation(t["rr"][:], pbank[4].ap()[:, 0:T], AF.Relu,
                                                       bias=t["pbdt"][:, dt_i:dt_i + 1]), after=[p3])
            return p4

        p4_next = emit_pchain(0, [tk_pe_phaseA, tk_crep[-1], ev_b] + tk_qxT[-1:])

        def dA_slot0(n):
            q = n % 4
            return (t["dA"].ap()[:, 0:T], t["dA"].ap()[:, T:2 * T],
                    t["dA2"][:, 0:T], t["dA2"][:, T:2 * T])[q]

        def emit_head(dt_n, p4_n, first=False):
            par_n = dt_n % 2
            mdtb_n = t["mdtB"] if par_n == 0 else t["mdtB1"]
            sg3_n = dve(lambda g, mdtb_n=mdtb_n: g.scalar_tensor_tensor(mdtb_n[:], t["ll"][:], -1.0, t["rr"][:],
                                                                        OP.mult, OP.subtract),
                        after=[p4_n] + tk_mdt_readers[par_n] + ([tk_state0] if (s == 0 and first) else []))
            tk_mdt_readers[par_n] = []
            exp0 = []
            for j2 in range(2):
                exk = act(lambda g, dt_n=dt_n, j2=j2, mdtb_n=mdtb_n:
                          g.activation(dA_slot0(j2), mdtb_n[:], AF.Exp,
                                       scale=t["negA"][:, dt_n * NST + j2: dt_n * NST + j2 + 1]),
                          after=[sg3_n, tk_dA_war[j2]])
                tk_mdt_readers[par_n].append(exk)
                exp0.append(exk)
            brep0 = []
            for j in range(2):
                brep0.append(pe(lambda g, j=j: g.matmul(pbank[6].ap()[:, j * T:(j + 1) * T],
                                                        t["dlt"][:, j * 128:(j + 1) * 128], t["sbBb"][:],
                                                        start=True, stop=True),
                                after=[ev_b, tk_dlt] + tk_brep_war[0] +
                                      ([tk_pe_phaseA] if first else [])))
            return sg3_n, exp0, brep0

        head_cur = emit_head(0, p4_next, first=True)
        for dt_i in range(NDT):
            par = dt_i % 2
            mdtb = t["mdtB"] if par == 0 else t["mdtB1"]
            p4 = p4_next
            tk_dt_war = [p4]
            sg3, exp0_cur, brep0_cur = head_cur
            tk_sp_war = sg3
            sg3p[par] = sg3
            v1t = dve(lambda g, dt_i=dt_i, mdtb=mdtb: g.tensor_tensor(t["vneg"][:], mdtb[:],
                                                                      t["u"][:, dt_i * T:(dt_i + 1) * T], OP.mult),
                      after=[sg3, tk_u[dt_i], tk_vneg_war])
            tk_mdt_readers[par].append(v1t)

            def pair_slot(gi):
                return t["hsc"].ap()[:, 0:2 * T] if gi % 2 == 0 else t["hscB"]

            def dA_slot(n):
                q = n % 4
                return (t["dA"].ap()[:, 0:T], t["dA"].ap()[:, T:2 * T],
                        t["dA2"][:, 0:T], t["dA2"][:, T:2 * T])[q]

            def emit_exp(gi2):
                toks = []
                for j2 in range(2):
                    n2 = gi2 * 2 + j2
                    dAb2 = dA_slot(n2)
                    exk = act(lambda g, dt_i=dt_i, n2=n2, dAb2=dAb2, mdtb=mdtb:
                              g.activation(dAb2, mdtb[:], AF.Exp,
                                           scale=t["negA"][:, dt_i * NST + n2: dt_i * NST + n2 + 1]),
                              after=[sg3, tk_dA_war[n2 % 4]])
                    tk_mdt_readers[par].append(exk)
                    toks.append(exk)
                return toks

            tk_exp_pair = exp0_cur
            for gi in range(NST // 2):
                bb = 6 + (gi % 2)
                if gi == 7 and dt_i + 1 < NDT:
                    head_cur = emit_head(dt_i + 1, p4_next)
                if gi == 0:
                    mmbr = brep0_cur
                else:
                    mmbr = []
                    for j in range(2):
                        n = gi * 2 + j
                        mmbr.append(pe(lambda g, n=n, j=j, bb=bb:
                                       g.matmul(pbank[bb].ap()[:, j * T:(j + 1) * T],
                                                t["dlt"][:, n * 128:(n + 1) * 128], t["sbBb"][:],
                                                start=True, stop=True),
                                       after=[ev_b, tk_dlt] + tk_brep_war[gi % 2] +
                                             ([tk_pe_phaseA] if dt_i == 0 and gi < 2 else [])))
                dbus = []
                for j in range(2):
                    dbuj = dve(lambda g, bb=bb, j=j:
                               g.tensor_tensor(pbank[5].ap()[:, j * T:(j + 1) * T],
                                               t["vneg"][:], pbank[bb].ap()[:, j * T:(j + 1) * T],
                                               OP.mult),
                               after=[mmbr[j], v1t, tk_dbu_war[j]] +
                                     ([tk_qxT[-1]] if (dt_i == 0 and gi == 0 and tk_qxT) else []))
                    dbus.append(dbuj)
                tk_brep_war[gi % 2] = dbus
                ps_ = pair_slot(gi)
                scs = []
                for j in range(2):
                    n = gi * 2 + j
                    hb = ps_[:, j * T:(j + 1) * T]
                    sc = dve(lambda g, hb=hb, dt_i=dt_i, n=n, j=j:
                             g.tensor_tensor_scan(hb, dA_slot(n), pbank[5].ap()[:, j * T:(j + 1) * T],
                                                  t["state"][:, dt_i * NST + n: dt_i * NST + n + 1],
                                                  OP.mult, OP.add),
                             after=[tk_exp_pair[j], dbus[j]] + (tk_h_war[gi % 2] or []))
                    tk_dA_war[n % 4] = sc
                    scs.append(sc)
                tk_dbu_war = scs
                if gi + 1 < NST // 2:
                    tk_exp_pair = emit_exp(gi + 1)
                stcs = []
                for j in range(2):
                    n = gi * 2 + j
                    hb = ps_[:, j * T:(j + 1) * T]
                    stc = act(lambda g, hb=hb, dt_i=dt_i, n=n:
                              g.copy(t["state"][:, dt_i * NST + n: dt_i * NST + n + 1], hb[:, T - 1:T]),
                              after=[scs[j]])
                    stcs.append(stc)
                n0 = gi * 2
                slot = gi % 2
                hcp0 = dve(lambda g, n0=n0, slot=slot, ps_=ps_: g.tensor_tensor(
                               t["gsml"].ap()[:, slot * 2 * T: slot * 2 * T + T], ps_[:, 0:T],
                               t["crep"][:, n0 * T:(n0 + 1) * T], OP.mult),
                           after=[scs[0], tk_crep[n0]] + (tk_hC_war[slot] or []))
                hcp1 = gp(lambda g, n0=n0, slot=slot, ps_=ps_: g.tensor_tensor(
                              t["gsml"].ap()[:, slot * 2 * T + T: slot * 2 * T + 2 * T], ps_[:, T:2 * T],
                              t["crep"][:, (n0 + 1) * T:(n0 + 2) * T], OP.mult),
                          after=[scs[1], tk_crep[n0 + 1]] + (tk_hC_war[slot] or []))
                tk_h_war[slot] = [hcp0, hcp1] + stcs
                if gi == 0:
                    a1_ = gp(lambda g, slot=slot: g.tensor_copy(t["yacc"][:], t["gsml"].ap()[:, slot * 2 * T: slot * 2 * T + T]),
                             after=[hcp0, tk_yaccA_war])
                else:
                    a1_ = gp(lambda g, slot=slot: g.tensor_tensor(t["yacc"][:], t["yacc"][:],
                                                                  t["gsml"].ap()[:, slot * 2 * T: slot * 2 * T + T], OP.add),
                             after=[hcp0, tk_yaccA_war])
                a2_ = gp(lambda g, slot=slot: g.tensor_tensor(t["yacc"][:], t["yacc"][:],
                                                              t["gsml"].ap()[:, slot * 2 * T + T: slot * 2 * T + 2 * T], OP.add),
                         after=[a1_, hcp1])
                tk_yaccA_war = a2_
                tk_hC_war[slot] = [a2_]
                if gi == 2 and dt_i + 1 < NDT:
                    p4_next = emit_pchain(dt_i + 1, tk_dt_war)

            du = dve(lambda g, dt_i=dt_i: g.scalar_tensor_tensor(t["hsc"][:, 0:T], t["u"][:, dt_i * T:(dt_i + 1) * T],
                                                                 t["dvec"][:, dt_i:dt_i + 1], t["yacc"][:],
                                                                 OP.mult, OP.add),
                     after=[tk_yaccA_war] + (tk_h_war[0] or []) + (tk_h_war[1] or []))
            tk_yaccA_war = du
            yg = dve(lambda g, dt_i=dt_i: g.tensor_tensor(t["yg"][:], t["hsc"][:, 0:T],
                                                          t["gz"][:, dt_i * T:(dt_i + 1) * T],
                                                          OP.mult), after=[du, tk_gz[dt_i], tk_yg_war])
            tk_h_war[0] = [yg]
            tk_hC_war[0] = (tk_hC_war[0] or []) + [yg]
            tk_hC_war[1] = (tk_hC_war[1] or []) + [yg]
            last_mm = None
            for ot in range(NDM):
                bk = ot // 2
                reg = pbank[bk][:, (ot % 2) * T:(ot % 2) * T + T]
                fb_ = bk not in mm4_first
                if fb_:
                    mm4_first.add(bk)
                last_mm = pe(lambda g, dt_i=dt_i, ot=ot, reg=reg, fb_=fb_:
                             g.matmul(reg, t["woT"][:, dt_i * c.DM + ot * 128: dt_i * c.DM + (ot + 1) * 128],
                                      t["yg"][:], start=fb_, stop=(dt_i == NDT - 1 and ot % 2 == 1)),
                             after=[yg] + ([tk_pe_phaseA, tk_crep[-1]] if dt_i == 0 and ot == 0 else []))
            tk_yg_war = last_mm
            tk_mm4_last = last_mm
        tk_dbu_war_final = list(tk_dbu_war)
        tk_dtbank_war = [tk_dt_war, []]

        sub_end = []
        if is_l0:
            prev_e = [None, None]
            tk_l0ev = []
            for ot in range(NDM):
                bk = ot // 2
                reg = pbank[bk][:, (ot % 2) * T:(ot % 2) * T + T]
                dst = t["h1st"] if ot % 2 == 0 else t["h1st2"]
                ev = act(lambda g, reg=reg, dst=dst: g.copy(dst[:], reg), after=[tk_mm4_last, prev_e[ot % 2]])
                tk_l0ev.append(ev)
                dmo = dma(dr["h1s"][ot * 128:(ot + 1) * 128, xoff:xoff + T], dst[:], after=[ev], ch="ch1")
                prev_e[ot % 2] = dmo
                sub_end.append(dmo)
        else:
            tk_tp = []
            tk_tp_bank = {}
            prev_e = [None, None]
            tk_b01_free = []
            for ot in range(NDM):
                bk = ot // 2
                reg = pbank[bk][:, (ot % 2) * T:(ot % 2) * T + T]
                h2b = t["h2"] if ot % 2 == 0 else t["mdtB"]
                ev = act(lambda g, reg=reg, h2b=h2b: g.copy(h2b[:], reg),
                         after=[tk_mm4_last, prev_e[ot % 2]] + tk_dt_war +
                               (tk_mdt_readers[0] + tk_mdt_readers[1] if ot < 2 else []))
                if ot < 4:
                    tk_b01_free.append(ev)
                for tt in range(TT):
                    tb = 4 + tt * 2 + (ot // 4)
                    col = (ot % 4) * 128
                    tp = pe(lambda g, tb=tb, col=col, tt=tt, h2b=h2b:
                            g.transpose(pbank[tb][:, col:col + 128], h2b[:, tt * 128:(tt + 1) * 128],
                                        t["idf"][:]),
                            after=[ev] + tk_dbu_war_final + tk_brep_war)
                    tk_tp.append(tp)
                    tk_tp_bank.setdefault(tb, []).append(tp)
                prev_e[ot % 2] = tk_tp[-1]
            tk_ev23 = [ev]
            tk_fs_last = []
            tk_o1s = []
            tk_o3s = []
            ep_prev = None
            gs_last = None
            for tt in range(TT):
                gtoks = []
                for half in range(2):
                    mmf = None
                    for dmt in range(NDM):
                        mmf = pe(lambda g, tt=tt, half=half, dmt=dmt:
                                 g.matmul(pbank[half][:, 0:512],
                                          t["qxT"][:, dmt * T + tt * 128: dmt * T + (tt + 1) * 128],
                                          t["w8bT"][:, dmt * c.DM + half * 512: dmt * c.DM + (half + 1) * 512],
                                          start=(dmt == 0), stop=(dmt == NDM - 1)),
                                 after=(tk_b01_free + [gs_last, ep_prev] + tk_qxT) if dmt == 0 else [])
                    fs = dve(lambda g, half=half, tt=tt:
                             g.scalar_tensor_tensor(t["fsc"][:], pbank[half][:, 0:512], t["wsxs"][:, tt:tt + 1],
                                                    t["brep"][:, half * 512:(half + 1) * 512], OP.mult, OP.add),
                             after=[mmf, tk_wsxs[tt], gs_last])
                    gs_ = act(lambda g, half=half:
                              g.activation(t["gsml"][:, half * 512:(half + 1) * 512],
                                           t["fsc"][:], AF.Silu), after=[fs])
                    gtoks.append(gs_)
                    gs_last = gs_
                    tk_fs_last = [fs]
                tk_xr = dma(t["x"][:, 0:c.DM], dr["x"][xoff + tt * 128: xoff + (tt + 1) * 128, :],
                            after=[tk_xn[1], ep_prev, tk_dA_war[2], tk_dA_war[3]] +
                                  (tk_h_war[1] or []), ch="cx")
                otoks = []
                for piece in range(2):
                    bkp = 4 + tt * 2 + piece
                    o1 = dve(lambda g, tt=tt, piece=piece, bkp=bkp:
                             g.tensor_tensor(t["ep1"][:, piece * 512:(piece + 1) * 512], pbank[bkp][:, 0:512],
                                             t["xn"][:, tt * c.DM + piece * 512: tt * c.DM + (piece + 1) * 512],
                                             OP.add),
                             after=tk_tp_bank[bkp] + [tk_xn[tt], ep_prev] + gtoks)
                    o2 = dve(lambda g, piece=piece:
                             g.tensor_tensor(t["ep1"][:, piece * 512:(piece + 1) * 512],
                                             t["ep1"][:, piece * 512:(piece + 1) * 512],
                                             t["gsml"][:, piece * 512:(piece + 1) * 512],
                                             OP.mult), after=[o1] + gtoks)
                    o3 = dve(lambda g, piece=piece:
                             g.tensor_tensor(t["ep1"][:, piece * 512:(piece + 1) * 512],
                                             t["ep1"][:, piece * 512:(piece + 1) * 512],
                                             t["x"][:, piece * 512:(piece + 1) * 512],
                                             OP.add), after=[o2, tk_xr])
                    otoks.append(o3)
                    tk_o1s.append(o1)
                    tk_o3s.append(o3)
                dmo = dma(dr["out"][xoff + tt * 128: xoff + (tt + 1) * 128, :], t["ep1"][:, 0:c.DM], after=otoks, ch="cout")
                ep_prev = dmo
                sub_end.append(dmo)
        nfine = {"full": sub_end}
        if is_l0:
            nfine["xfree"] = [tk_xn[-1]]
            nfine["o1s"] = []
            nfine["hinfree"] = [tk_xz_last]
            nfine["b01"] = [tk_l0ev[3]]
            nfine["b23"] = [tk_l0ev[7]]
            nfine["b4567"] = tk_dbu_war_final + tk_brep_war + tk_dt_war
        else:
            nfine["xfree"] = [tk_o3s[-1]]
            nfine["o1s"] = tk_o1s
            nfine["hinfree"] = [tk_xz_last]
            nfine["b01"] = tk_fs_last
            nfine["b23"] = tk_ev23
            nfine["b4567"] = tk_o1s[-1:]
            nfine["qfree"] = [sub_end[-1]]
        nfine["crepfree"] = tk_hC_war[0] + tk_hC_war[1]
        nfine["dafree"] = [tk_dA_war[0], tk_dA_war[1], tk_dA_war[2], tk_dA_war[3]]
        nfine["ufree"] = [tk_yaccA_war, tk_yg_war]
        return nfine

    w0 = plan_layer_weights(0, after=[tk_flag])
    fine = {"full": w0 + [tk_state0, tk_tails0, tk_dlt, tk_idb]}
    for s in range(c.NSUB):
        fine = plan_subtile(0, s, fine)
    prev = fine["full"] + fine["dafree"] + fine["ufree"] + fine["crepfree"] + fine["b4567"] + fine["hinfree"]
    tk_s1 = dve(lambda g: g.memset(t["state"][:], 0.0), after=prev)
    tk_t1 = dve(lambda g: g.memset(t["tails"][:], 0.0), after=prev)
    w1 = plan_layer_weights(1, after=prev + [tk_s1, tk_t1])
    wf = plan_ffn_weights(after=prev + [w1[-1]])
    fine = {"full": w1 + wf + [tk_s1, tk_t1]}
    for s in range(c.NSUB):
        fine = plan_subtile(1, s, fine)

    finals = [(ch, pl.cnt[ch]) for ch in DMACH if pl.cnt[ch] > 0]
    pl.finalize()

    with nc.Block() as block:
        @block.sync
        def _(sync):
            pl.replay_all("sync", sync, sems)
            for ch, v in finals:
                sync.wait_ge(sems[ch], v)

        @block.tensor
        def _(tensor):
            pl.replay_all("tensor", tensor, sems)

        @block.vector
        def _(vector):
            pl.replay_all("vector", vector, sems)

        @block.scalar
        def _(scalar):
            pl.replay_all("scalar", scalar, sems)

        @block.gpsimd
        def _(gpsimd):
            pl.replay_all("gpsimd", gpsimd, sems)

    S.close()
    return nc, dr


def prep_inputs(core_x, ins, cfg):
    c = cfg
    d = {"x": np.ascontiguousarray(core_x)}
    for l in range(2):
        d[f"wiT{l}"] = np.ascontiguousarray(ins["L0_attn_W_in"][l].T)
        d[f"woT{l}"] = np.ascontiguousarray(ins["L0_attn_W_out"][l].T)
        wx = ins["L0_attn_W_x"][l]
        d[f"wxdt{l}"] = np.ascontiguousarray(wx[:c.RL].T)
        d[f"wxB{l}"] = np.ascontiguousarray(wx[c.RL:c.RL + c.NST].T)
        d[f"wxC{l}"] = np.ascontiguousarray(wx[c.RL + c.NST:].T)
        d[f"wdtT{l}"] = np.ascontiguousarray(ins["L0_attn_W_dt"][l].T)
        d[f"alog{l}"] = np.ascontiguousarray(
            ins["L0_attn_A_log"][l].reshape(c.NDT, 128, c.NST).transpose(1, 0, 2).reshape(128, c.NDT * c.NST))
        d[f"convw{l}"] = np.ascontiguousarray(
            ins["L0_attn_conv_w"][l].reshape(c.NDT, 128, 4).transpose(1, 0, 2).reshape(128, c.NDT * 4))
        d[f"convb{l}"] = np.ascontiguousarray(ins["L0_attn_conv_b"][l].reshape(c.NDT, 128).T)
        d[f"dvec{l}"] = np.ascontiguousarray(ins["L0_attn_D"][l].reshape(c.NDT, 128).T)
        d[f"bdt{l}"] = np.ascontiguousarray(ins["L0_attn_b_dt"][l].reshape(c.NDT, 128).T)
    d["w8T"] = np.ascontiguousarray(np.asarray(ins["L0_ffn_weight"]).view(np.uint8).T)
    d["fbias"] = np.ascontiguousarray(np.asarray(ins["L0_ffn_bias"], np.float32).reshape(1, -1))
    d["wscale"] = np.asarray(ins["L0_ffn_weight_scale"], np.float32).reshape(1, 1)
    return d




_CFG = Cfg(L=4096, T=256)
_NC_CACHE = {}


def _build_program():
    if "nc" not in _NC_CACHE:
        nc = bass.Bass("TRN2", target_bir_lowering=False)
        nc, dr = build(_CFG, nc)
        _NC_CACHE["nc"] = nc
    return _NC_CACHE["nc"]


def kernel(L0_attn_x, L0_attn_W_in, L0_attn_conv_w, L0_attn_conv_b, L0_attn_W_x,
           L0_attn_W_dt, L0_attn_b_dt, L0_attn_A_log, L0_attn_D, L0_attn_W_out,
           L0_ffn_weight, L0_ffn_weight_scale, L0_ffn_bias):
    from concourse.bass_utils import run_bass_kernel_spmd
    ins = dict(L0_attn_W_in=np.asarray(L0_attn_W_in, np.float32),
               L0_attn_conv_w=np.asarray(L0_attn_conv_w, np.float32),
               L0_attn_conv_b=np.asarray(L0_attn_conv_b, np.float32),
               L0_attn_W_x=np.asarray(L0_attn_W_x, np.float32),
               L0_attn_W_dt=np.asarray(L0_attn_W_dt, np.float32),
               L0_attn_b_dt=np.asarray(L0_attn_b_dt, np.float32),
               L0_attn_A_log=np.asarray(L0_attn_A_log, np.float32),
               L0_attn_D=np.asarray(L0_attn_D, np.float32),
               L0_attn_W_out=np.asarray(L0_attn_W_out, np.float32),
               L0_ffn_weight=L0_ffn_weight,
               L0_ffn_weight_scale=L0_ffn_weight_scale,
               L0_ffn_bias=np.asarray(L0_ffn_bias, np.float32))
    x = np.asarray(L0_attn_x, np.float32)
    nc = _build_program()
    wshared = prep_inputs(x[0], ins, _CFG)
    in_maps = []
    for b in range(8):
        d = dict(wshared)
        d["x"] = np.ascontiguousarray(x[b])
        in_maps.append(d)
    res = run_bass_kernel_spmd(nc, in_maps, core_ids=list(range(8)))
    out = np.stack([np.asarray(r["out"], np.float32) for r in res.results], axis=0)
    return out


# version 28
# speedup vs baseline: 1.0161x; 1.0161x over previous
import numpy as np
from contextlib import ExitStack
import concourse.bass as bass
import concourse.mybir as mybir

F32 = mybir.dt.float32
BF16 = mybir.dt.bfloat16
U8 = mybir.dt.uint8
AF = mybir.ActivationFunctionType
OP = mybir.AluOpType
AX = mybir.AxisListType

ENGINES = ("sync", "tensor", "vector", "scalar", "gpsimd")
DMACH = ("cx", "chin", "cw", "ch1", "cout")

M64 = 1.5 * 2.0 ** 52
M32 = 1.5 * 2.0 ** 23
LN2 = float(np.log(2.0))


class Cfg:
    def __init__(self, L=4096, T=256):
        self.L = L
        self.T = T
        self.NSUB = L // T
        self.DM = 1024
        self.E = 4096
        self.DI = 2048
        self.NST = 16
        self.RL = 64
        self.NDT = self.DI // 128
        self.NDM = self.DM // 128
        self.NE = self.E // 128
        self.TT = T // 128
        self.NG = 4


class Plan:
    def __init__(self):
        self.recs = {e: [] for e in ENGINES}
        self.cnt = {e: 0 for e in ENGINES}
        for ch in DMACH:
            self.cnt[ch] = 0

    def op(self, eng, fn, after=(), dma=False, ch=None):
        waits = []
        for tok in after:
            if tok is None:
                continue
            if isinstance(tok, list):
                for tk in tok:
                    if tk is not None:
                        waits.append(tk)
                continue
            waits.append(tok)
        if dma:
            assert ch in DMACH
            if self.cnt[ch] > 0:
                waits.append((ch, self.cnt[ch]))
            self.cnt[ch] += 16
            self.recs[eng].append((fn, waits, (ch, 16)))
            return (ch, self.cnt[ch])
        self.cnt[eng] += 1
        self.recs[eng].append((fn, waits, (eng, 1)))
        return (eng, self.cnt[eng])

    def tok(self, eng):
        return (eng, self.cnt[eng])

    def finalize(self):
        marks = {e: set() for e in ENGINES}
        for e in ENGINES:
            for fn, waits, (isem, inc) in self.recs[e]:
                for (e2, v2) in waits:
                    if e2 in marks:
                        marks[e2].add(v2)
        remap = {}
        for e in ENGINES:
            sorted_marks = sorted(marks[e])
            remap[e] = {v: i + 1 for i, v in enumerate(sorted_marks)}
        self._remap = remap
        self._marks = marks

    def replay_all(self, eng, engobj, sems):
        emitted = 0
        waited = {}
        for fn, waits, (isem, inc) in self.recs[eng]:
            for (e2, v2) in waits:
                if e2 in self._remap:
                    v2r = self._remap[e2][v2]
                else:
                    v2r = v2
                if e2 == eng and v2 > emitted:
                    raise RuntimeError(f"self-wait deadlock on {eng}: need {v2}, at {emitted}")
                if waited.get(e2, -1) < v2r:
                    engobj.wait_ge(sems[e2], v2r)
                    waited[e2] = v2r
            inst = fn(engobj)
            if isem != eng:
                inst.then_inc(sems[isem], inc)
            else:
                emitted += inc
                if emitted in self._marks[eng]:
                    inst.then_inc(sems[eng], 1)


def build(cfg, nc):
    c = cfg
    T, TT, NDT, NDM, NE, NST, NG = c.T, c.TT, c.NDT, c.NDM, c.NE, c.NST, c.NG

    dr = {}
    dr["x"] = nc.dram_tensor("x", [c.L, c.DM], F32, kind="ExternalInput").ap()
    for l in range(2):
        dr[f"wiT{l}"] = nc.dram_tensor(f"wiT{l}", [c.DM, c.E], F32, kind="ExternalInput").ap()
        dr[f"woT{l}"] = nc.dram_tensor(f"woT{l}", [c.DI, c.DM], F32, kind="ExternalInput").ap()
        dr[f"wxdt{l}"] = nc.dram_tensor(f"wxdt{l}", [c.DI, c.RL], F32, kind="ExternalInput").ap()
        dr[f"wxB{l}"] = nc.dram_tensor(f"wxB{l}", [c.DI, NST], F32, kind="ExternalInput").ap()
        dr[f"wxC{l}"] = nc.dram_tensor(f"wxC{l}", [c.DI, NST], F32, kind="ExternalInput").ap()
        dr[f"wdtT{l}"] = nc.dram_tensor(f"wdtT{l}", [c.RL, c.DI], F32, kind="ExternalInput").ap()
        dr[f"alog{l}"] = nc.dram_tensor(f"alog{l}", [128, NDT * NST], F32, kind="ExternalInput").ap()
        dr[f"convw{l}"] = nc.dram_tensor(f"convw{l}", [128, NDT * 4], F32, kind="ExternalInput").ap()
        dr[f"convb{l}"] = nc.dram_tensor(f"convb{l}", [128, NDT], F32, kind="ExternalInput").ap()
        dr[f"dvec{l}"] = nc.dram_tensor(f"dvec{l}", [128, NDT], F32, kind="ExternalInput").ap()
        dr[f"bdt{l}"] = nc.dram_tensor(f"bdt{l}", [128, NDT], F32, kind="ExternalInput").ap()
    dr["w8T"] = nc.dram_tensor("w8T", [c.DM, c.DM], U8, kind="ExternalInput").ap()
    dr["fbias"] = nc.dram_tensor("fbias", [1, c.DM], F32, kind="ExternalInput").ap()
    dr["wscale"] = nc.dram_tensor("wscale", [1, 1], F32, kind="ExternalInput").ap()
    dr["out"] = nc.dram_tensor("out", [c.L, c.DM], F32, kind="ExternalOutput").ap()
    dr["h1s"] = nc.dram_tensor("h1s", [c.DM, c.L], BF16, kind="ExternalOutput").ap()

    S = ExitStack()
    sb = lambda n, sh, d=F32: S.enter_context(nc.sbuf_tensor(n, sh, d))
    psb = lambda n, sh, d=F32: S.enter_context(nc.psum_tensor(n, sh, d))

    t = {}
    t["wiT"] = sb("wiT", [128, NDM * c.E], BF16)
    t["woT"] = sb("woT", [128, NDT * c.DM], BF16)
    t["wxdt"] = sb("wxdt", [128, NDT * c.RL])
    t["wxB"] = sb("wxB", [128, NDT * NST])
    t["wxC"] = sb("wxC", [128, NDT * NST])
    t["wdtT"] = sb("wdtT", [64, NDT * 128], BF16)
    t["negA"] = sb("negA", [128, NDT * NST])
    t["convw"] = sb("convw", [128, NDT * 4])
    t["convb"] = sb("convb", [128, NDT])
    t["dvec"] = sb("dvec", [128, NDT])
    t["pbdt"] = sb("pbdt", [128, NDT])
    t["w8bT"] = sb("w8bT", [128, NDM * c.DM], mybir.dt.float8e4)
    t["u8s"] = sb("u8s", [128, c.DM // 2], U8)
    t["brep"] = sb("brep", [128, c.DM])
    t["wsrep"] = sb("wsrep", [128, 1])
    t["xn"] = sb("xn", [128, TT * c.DM])
    t["hin"] = sb("hin", [128, NDM * T], BF16)
    t["u"] = sb("ubuf", [128, NDT * T])
    t["gz"] = sb("gz", [128, NDT * T])
    t["crep"] = sb("crep", [128, NST * T], BF16)

    t["tails"] = sb("tails", [128, NDT * 3])
    t["cacc"] = sb("cacc", [128, T])
    t["dtlow"] = sb("dtlow", [64, T], BF16)
    t["sbCb"] = sb("sbCb", [16, T], BF16)
    t["sbBb"] = sb("sbBb", [16, T], BF16)
    t["mdtB"] = sb("mdtB", [128, T])
    t["vneg"] = sb("vneg", [128, T])
    t["slab"] = sb("slab", [128, TT * NG * 128])
    t["dA"] = sb("dA", [128, 2 * T])
    t["hsc"] = sb("hsc", [128, 2 * T])
    t["hC"] = sb("hCs", [128, T])
    t["yacc"] = sb("yacc", [128, T])
    t["yg"] = sb("ygb", [128, T], BF16)
    t["state"] = sb("state", [128, NDT * NST])
    t["idf"] = sb("idf", [128, 128])
    t["idb"] = sb("idb", [128, 128], BF16)
    t["dlt"] = sb("dlt", [16, NST * 128], mybir.dt.float8e4)
    t["flag"] = sb("flag", [128, 1])
    t["stat"] = sb("stat", [128, 16])
    t["qxT"] = sb("qxT", [128, NDM * T], BF16)
    t["gsml"] = sb("gsml", [128, c.DM])
    t["q1"] = sb("q1", [128, c.DM])
    t["q2"] = sb("q2", [128, c.DM])

    t["xs"] = sb("xs", [128, TT])
    t["wsxs"] = sb("wsxs", [128, TT])
    t["h1st2"] = sb("h1st2", [128, T], BF16)
    t["aa"] = t["cacc"].ap()[:, 0:T]
    t["ll"] = t["q1"].ap()[:, 0:T]
    t["rr"] = t["q1"].ap()[:, T:2 * T]
    t["mdtB1"] = t["q1"].ap()[:, 2 * T:3 * T]
    t["hscB"] = t["q2"].ap()[:, 0:2 * T]
    t["dA2"] = t["q2"].ap()[:, 2 * T:4 * T]
    t["sbB"] = t["mdtB"].ap()[0:16, 0:T]
    t["h2"] = t["vneg"].ap()
    t["h1st"] = t["yg"].ap()
    t["qxs2"] = t["yg"].ap()[:, 128:256]
    t["q3"] = t["gsml"].ap()[:, 0:c.DM]
    t["fsc2"] = t["slab"].ap()[:, 0:c.DM]
    t["ep1"] = t["q1"].ap()
    t["fsc"] = t["hsc"].ap()[:, 0:512]
    t["x"] = t["q2"]
    t["uraw"] = t["dA"].ap()[:, 0:T + 3]
    t["qxs"] = t["yg"].ap()[:, 0:128]

    pbank = [psb(f"pb{i}", [128, 512]) for i in range(8)]

    sems = {e: S.enter_context(nc.semaphore("sem_" + e)) for e in ENGINES}
    for ch in DMACH:
        sems[ch] = S.enter_context(nc.semaphore("sem_" + ch))

    pl = Plan()

    def dma(dst, src, after=(), ch="cw"):
        return pl.op("sync", lambda g, dst=dst, src=src: g.dma_start(out=dst, in_=src), after=after, dma=True, ch=ch)

    def act(fn, after=()):
        return pl.op("scalar", fn, after=after)

    def dve(fn, after=()):
        return pl.op("vector", fn, after=after)

    def pe(fn, after=()):
        return pl.op("tensor", fn, after=after)

    def gp(fn, after=()):
        return pl.op("gpsimd", fn, after=after)

    tk_iota = gp(lambda g: g.iota(t["idf"][:], [[1, 128]], base=0, channel_multiplier=-1,
                                  allow_small_or_imprecise_dtypes=True))
    tk_idf = gp(lambda g: g.tensor_scalar(t["idf"][:], t["idf"][:], 0.0, None, OP.is_equal), after=[tk_iota])
    tk_idb = dve(lambda g: g.tensor_copy(t["idb"][:], t["idf"][:]), after=[tk_idf])
    tk_rowg = gp(lambda g: g.iota(t["dlt"].ap().rearrange("j (n p) -> j n p", n=NST),
                                  [[0, NST], [0, 128]], base=0, channel_multiplier=1,
                                  allow_small_or_imprecise_dtypes=True), after=[tk_idf])
    tk_dlt = None
    for n in range(NST):
        tk_dlt = gp(lambda g, n=n: g.tensor_scalar(t["dlt"][:, n * 128:(n + 1) * 128],
                                                   t["dlt"][:, n * 128:(n + 1) * 128],
                                                   float(n), None, OP.is_equal), after=[tk_rowg])
    tk_f0 = dve(lambda g: g.memset(t["flag"][:], 1.0))
    tk_f1 = dve(lambda g: g.tensor_scalar(t["flag"][:], t["flag"][:], 2.0 ** -30, -1.0, OP.add, OP.add), after=[tk_f0])
    tk_flag = dve(lambda g: g.tensor_scalar(t["flag"][:], t["flag"][:], 0.0, None, OP.is_gt), after=[tk_f1])
    tk_state0 = dve(lambda g: g.memset(t["state"][:], 0.0))
    tk_cb1 = dve(lambda g: g.memset(t["stat"][:, 5:6], 1e-5))
    tk_cb2 = dve(lambda g: g.memset(t["stat"][:, 6:7], 1.0))
    tk_cb3 = dve(lambda g: g.memset(t["stat"][:, 7:8], 9.0 * LN2))
    tk_cb4 = dve(lambda g: g.memset(t["stat"][:, 13:14], 5.0 * LN2))
    tk_tails0 = dve(lambda g: g.memset(t["tails"][:], 0.0))

    def plan_rint(ap_get, after, pre_mul=None, pre_add=None, scratch="q3"):
        a_ = ap_get
        w = a_().shape[-1]
        sc = lambda: t[scratch][:, :w]
        deps = list(after)
        if pre_mul is not None:
            pm, pa = float(pre_mul), float(0.0 if pre_add is None else pre_add)
            t0 = dve(lambda g: g.tensor_scalar(a_(), a_(), pm, pa, OP.mult, OP.add), after=deps)
            deps = [t0]
        t1 = dve(lambda g: g.tensor_scalar(sc(), a_(), M32, -M32, OP.add, OP.add), after=deps)
        t2 = dve(lambda g: g.tensor_scalar(a_(), a_(), M64, -M64, OP.add, OP.add), after=deps + [t1])
        t3 = dve(lambda g: g.tensor_tensor(a_(), a_(), sc(), OP.subtract), after=[t1, t2])
        t4 = dve(lambda g: g.scalar_tensor_tensor(a_(), a_(), t["flag"][:], sc(), OP.mult, OP.add),
                 after=[t3, tk_flag])
        return t4

    def plan_layer_weights(l, after=()):
        toks = []
        stage = t["u"]
        prev = list(after)
        for dmt in range(NDM):
            tk_d = dma(stage[:, :c.E], dr[f"wiT{l}"][dmt * 128:(dmt + 1) * 128, :], after=prev)
            tk_c = act(lambda g, dmt=dmt: g.copy(t["wiT"][:, dmt * c.E:(dmt + 1) * c.E], stage[:, :c.E]), after=[tk_d])
            prev = [tk_c]
            toks.append(tk_c)
        for dt_i in range(NDT):
            tk_d = dma(stage[:, :c.DM], dr[f"woT{l}"][dt_i * 128:(dt_i + 1) * 128, :], after=prev)
            tk_c = act(lambda g, dt_i=dt_i: g.copy(t["woT"][:, dt_i * c.DM:(dt_i + 1) * c.DM], stage[:, :c.DM]), after=[tk_d])
            prev = [tk_c]
            toks.append(tk_c)
        toks.append(dma(t["wxdt"].ap().rearrange("p (n r) -> p n r", r=c.RL),
                        dr[f"wxdt{l}"].rearrange("(n p) r -> p n r", p=128), after=after))
        toks.append(dma(t["wxB"].ap().rearrange("p (n r) -> p n r", r=NST),
                        dr[f"wxB{l}"].rearrange("(n p) r -> p n r", p=128), after=after))
        toks.append(dma(t["wxC"].ap().rearrange("p (n r) -> p n r", r=NST),
                        dr[f"wxC{l}"].rearrange("(n p) r -> p n r", p=128), after=after))
        tk_wd = dma(stage[0:64, :c.DI], dr[f"wdtT{l}"], after=prev)
        tk_wdc = act(lambda g: g.copy(t["wdtT"][:], stage[0:64, :c.DI]), after=[tk_wd])
        prev = [tk_wdc]
        toks.append(tk_wdc)
        tk_al = dma(t["negA"][:], dr[f"alog{l}"], after=after)
        toks.append(act(lambda g: g.activation(t["negA"][:], t["negA"][:], AF.Exp), after=[tk_al]))
        toks.append(dma(t["convw"][:], dr[f"convw{l}"], after=after))
        toks.append(dma(t["convb"][:], dr[f"convb{l}"], after=after))
        toks.append(dma(t["dvec"][:], dr[f"dvec{l}"], after=after))
        toks.append(dma(t["pbdt"][:], dr[f"bdt{l}"], after=after))
        return toks

    def plan_ffn_weights(after=()):
        toks = []
        fb = t["xn"].ap()[:, 0:c.DM]
        fe = t["q2"].ap()[:, :c.DM]
        sgn = t["q1"].ap()[:, :c.DM]
        prev = list(after)
        for dmt in range(NDM):
            kd0 = dma(t["u8s"][:], dr["w8T"][dmt * 128:(dmt + 1) * 128, 0:c.DM // 2], after=prev)
            k10 = dve(lambda g: g.tensor_copy(fb[:, 0:c.DM // 2], t["u8s"][:]), after=[kd0] + prev)
            kd1 = dma(t["u8s"][:], dr["w8T"][dmt * 128:(dmt + 1) * 128, c.DM // 2:], after=[k10])
            k1 = dve(lambda g: g.tensor_copy(fb[:, c.DM // 2:], t["u8s"][:]), after=[kd1])
            k2 = dve(lambda g: g.tensor_scalar(fe, fb, 128.0, None, OP.is_ge), after=[k1])
            k3 = dve(lambda g: g.scalar_tensor_tensor(fb, fe, -128.0, fb, OP.mult, OP.add), after=[k2])
            k4 = dve(lambda g: g.tensor_scalar(sgn, fe, -2.0, 1.0, OP.mult, OP.add), after=[k3])
            k5 = dve(lambda g: g.tensor_copy(fe, fb), after=[k4])
            k6 = plan_rint(lambda: fe, [k5], pre_mul=0.125, pre_add=-0.4375)
            k7 = dve(lambda g: g.scalar_tensor_tensor(fb, fe, -8.0, fb, OP.mult, OP.add), after=[k6])
            k8 = dve(lambda g: g.tensor_scalar(fb, fb, 8.0, None, OP.add), after=[k7])
            k9 = dve(lambda g: g.tensor_scalar(t["q3"][:, :c.DM], fe, 0.0, None, OP.is_gt), after=[k6, k8])
            k10 = act(lambda g: g.activation(fe, fe, AF.Exp, scale=LN2), after=[k9])
            k11 = plan_rint(lambda: fe, [k10], scratch="fsc2")
            k12 = dve(lambda g: g.tensor_tensor(fe, fe, fb, OP.mult), after=[k11])
            k13 = dve(lambda g: g.tensor_scalar(fe, fe, 1.0 / 2048, None, OP.mult), after=[k12])
            k14 = dve(lambda g: g.tensor_scalar(fb, fb, -8.0, 1.0 / 1024, OP.add, OP.mult), after=[k13])
            k15 = dve(lambda g: g.tensor_tensor(fe, fe, fb, OP.subtract), after=[k14])
            k16 = dve(lambda g: g.tensor_tensor(fe, fe, t["q3"][:, :c.DM], OP.mult), after=[k15])
            k17 = dve(lambda g: g.tensor_tensor(fe, fe, fb, OP.add), after=[k16])
            k18 = dve(lambda g, dmt=dmt: g.tensor_tensor(t["w8bT"][:, dmt * c.DM:(dmt + 1) * c.DM],
                                                         fe, sgn, OP.mult), after=[k17])
            prev = [k18]
            toks.append(k18)
        tk_fb2 = dma(t["q1"][0:1, :c.DM], dr["fbias"], after=prev)
        tk_ws = dma(t["q1"][0:1, c.DM - 1:c.DM], dr["wscale"], after=[tk_fb2])
        tk_on = dve(lambda g: g.memset(t["vneg"][0:1, 0:128], 1.0), after=prev)
        ones = t["vneg"].ap()[0:1, 0:128]
        m1 = pe(lambda g, ones=ones: g.matmul(pbank[0][:, 0:512], ones, t["q1"][0:1, 0:512], start=True, stop=True),
                after=[tk_fb2, tk_ws, tk_on] + list(after))
        m2 = pe(lambda g, ones=ones: g.matmul(pbank[1][:, 0:512], ones, t["q1"][0:1, 512:1024], start=True, stop=True),
                after=[m1])
        toks.append(dve(lambda g: g.tensor_copy(t["brep"][:, 0:512], pbank[0][:, 0:512]), after=[m1]))
        toks.append(dve(lambda g: g.tensor_copy(t["brep"][:, 512:1024], pbank[1][:, 0:512]), after=[m2]))
        toks.append(dve(lambda g: g.tensor_scalar(t["wsrep"][:], t["brep"][:, c.DM - 1:c.DM], 2.0, None, OP.mult),
                        after=[toks[-1]]))
        tk_fb3 = dma(t["q1"][0:1, 0:1], dr["fbias"][0:1, c.DM - 1:c.DM], after=[tk_ws] + toks[-1:])
        m3 = pe(lambda g, ones=ones: g.matmul(pbank[2][:, 0:1], ones, t["q1"][0:1, 0:1], start=True, stop=True),
                after=[tk_fb3, m2])
        toks.append(dve(lambda g: g.tensor_copy(t["brep"][:, c.DM - 1:c.DM], pbank[2][:, 0:1]), after=[m3]))
        return toks

    def plan_subtile(l, s, fine):
        is_l0 = (l == 0)
        xoff = s * T
        sq = t["q1"]
        prev = fine["full"]
        f_x = fine.get("xfree", prev)
        f_o1 = fine.get("o1s", prev)
        f_hin = fine.get("hinfree", prev)
        f_b01 = fine.get("b01", prev)
        f_b23 = fine.get("b23", prev)
        f_b4567 = fine.get("b4567", prev)
        f_da = fine.get("dafree", prev)
        f_u = fine.get("ufree", prev)
        f_crep = fine.get("crepfree", prev)
        f_q = fine.get("qfree", prev)

        tk_x = []
        tk_hin = []
        if not is_l0:
            for dmt in range(NDM):
                tk_hin.append(dma(t["hin"][:, dmt * T:(dmt + 1) * T],
                                  dr["h1s"][dmt * 128:(dmt + 1) * 128, xoff:xoff + T],
                                  after=f_hin, ch="chin"))

        tk_xn = []
        for tt in range(TT):
            xtt = t["x"][:, 0:c.DM]
            tk_xld = dma(t["x"][:, 0:c.DM], dr["x"][xoff + tt * 128: xoff + (tt + 1) * 128, :],
                         after=f_x + (tk_xn[-1:] if tk_xn else []), ch="cx")
            tk_x.append(tk_xld)
            s0 = t["stat"][:, tt * 8 + 0: tt * 8 + 1]
            s1 = t["stat"][:, tt * 8 + 1: tt * 8 + 2]
            s2 = t["stat"][:, tt * 8 + 2: tt * 8 + 3]
            s3 = t["stat"][:, tt * 8 + 3: tt * 8 + 4]
            a1 = dve(lambda g, xtt=xtt, s0=s0: g.tensor_reduce(s0, xtt, AX.X, OP.add), after=[tk_x[tt]])
            a2 = dve(lambda g, xtt=xtt: g.tensor_tensor(sq[:, :c.DM], xtt, xtt, OP.mult), after=[tk_x[tt]] + f_q)
            tk_x[tt] = tk_x[tt]
            a3 = dve(lambda g, s1=s1: g.tensor_reduce(s1, sq[:, :c.DM], AX.X, OP.add), after=[a2])
            a4 = dve(lambda g, s0=s0: g.tensor_scalar(s0, s0, 1.0 / c.DM, None, OP.mult), after=[a1])
            a5 = dve(lambda g, s1=s1: g.tensor_scalar(s1, s1, 1.0 / c.DM, None, OP.mult), after=[a3])
            a6 = dve(lambda g, s0=s0, s2=s2: g.tensor_tensor(s2, s0, s0, OP.mult), after=[a4])
            a7 = dve(lambda g, s1=s1, s2=s2: g.tensor_tensor(s2, s1, s2, OP.subtract), after=[a5, a6])
            a8 = act(lambda g, s2=s2, s3=s3: g.activation(s3, s2, AF.Ln, bias=t["stat"][:, 5:6]), after=[a7, tk_cb1])
            a9 = act(lambda g, s3=s3: g.activation(s3, s3, AF.Exp, scale=-0.5), after=[a8])
            a10 = dve(lambda g, s1=s1, s3=s3: g.tensor_tensor(s1, s3, s3, OP.mult), after=[a9])
            a11 = dve(lambda g, s2=s2: g.tensor_scalar(s2, s2, 1e-5, None, OP.add), after=[a10, a7])
            a12 = dve(lambda g, s1=s1, s2=s2: g.tensor_tensor(s1, s1, s2, OP.mult), after=[a10, a11])
            a13 = dve(lambda g, s1=s1: g.tensor_scalar(s1, s1, -0.5, 1.5, OP.mult, OP.add), after=[a12])
            a14 = dve(lambda g, s1=s1, s3=s3: g.tensor_tensor(s3, s3, s1, OP.mult), after=[a13])
            a15 = dve(lambda g, s0=s0, s3=s3: g.tensor_tensor(s0, s0, s3, OP.mult), after=[a14, a4])
            a16 = dve(lambda g, s0=s0: g.tensor_scalar(s0, s0, -1.0, None, OP.mult), after=[a15])
            a17 = act(lambda g, xtt=xtt, tt=tt, s0=s0, s3=s3:
                      g.activation(t["xn"][:, tt * c.DM:(tt + 1) * c.DM], xtt, AF.Identity, scale=s3, bias=s0),
                      after=[a16] + f_o1)
            tk_xn.append(a17)

        tk_wsxs = [None, None]
        tk_q2done = [None, None]
        if not is_l0:
            qprev = list(f_q)
            for tt in range(TT):
                xn_tt = t["xn"][:, tt * c.DM:(tt + 1) * c.DM]
                xs_c = t["xs"][:, tt:tt + 1]
                q1 = t["q1"].ap()[:, :c.DM]
                q2 = t["q2"].ap()[:, :c.DM]
                rxs = t["stat"][:, tt * 8 + 4: tt * 8 + 5]
                b1 = dve(lambda g, xn_tt=xn_tt, xs_c=xs_c: g.tensor_reduce(xs_c, xn_tt, AX.X, OP.max,
                                                                           apply_absolute_value=True),
                         after=[tk_xn[tt], tk_xn[1]] + qprev)
                b2 = dve(lambda g, xs_c=xs_c: g.tensor_scalar(xs_c, xs_c, 1e-12, 1.0 / 448.0, OP.max, OP.mult),
                         after=[b1])
                b3 = dve(lambda g, xs_c=xs_c, tt=tt: g.tensor_tensor(t["wsxs"][:, tt:tt + 1], xs_c, t["wsrep"][:],
                                                                     OP.mult), after=[b2])
                tk_wsxs[tt] = b3
                b4 = dve(lambda g, xs_c=xs_c, rxs=rxs: g.reciprocal(rxs, xs_c), after=[b2])
                b5 = dve(lambda g, xn_tt=xn_tt, rxs=rxs: g.tensor_scalar(q1, xn_tt, rxs, None, OP.mult),
                         after=[b4] + qprev)
                b6 = dve(lambda g: g.tensor_tensor(q2, q1, q1, OP.mult), after=[b5] + qprev)
                b7 = dve(lambda g: g.tensor_scalar(q2, q2, 1e-30, None, OP.max), after=[b6])
                b8 = act(lambda g: g.activation(q2, q2, AF.Ln), after=[b7])
                b9 = plan_rint(lambda: q2, [b8], pre_mul=0.5 / LN2, pre_add=-0.5)
                b10 = dve(lambda g: g.tensor_scalar(q2, q2, -6.0, -3.0, OP.max, OP.add), after=[b9])
                b11 = act(lambda g: g.activation(t["fsc2"][:, :c.DM], q2, AF.Exp, scale=LN2, bias=t["stat"][:, 7:8]),
                          after=[b10, tk_cb3] + qprev)
                b12 = plan_rint(lambda: t["fsc2"][:, :c.DM], [b11])
                b13 = act(lambda g: g.activation(q2, q2, AF.Exp, scale=-LN2, bias=t["stat"][:, 13:14]), after=[b12, b10, tk_cb4])
                b14 = plan_rint(lambda: q2, [b13])
                b15 = dve(lambda g: g.tensor_tensor(q2, q1, q2, OP.mult), after=[b14])
                b16 = plan_rint(lambda: q2, [b15], pre_mul=2.0 ** -5)
                b17 = dve(lambda g: g.tensor_tensor(q2, q2, t["fsc2"][:, :c.DM], OP.mult), after=[b16])
                if tt == 0:
                    m1_ = dve(lambda g: g.tensor_copy(t["hsc"][:], q2[:, 0:512]), after=[b17])
                    m2_ = dve(lambda g: g.tensor_copy(t["vneg"][:], q2[:, 512:768]), after=[b17])
                    m3_ = dve(lambda g: g.tensor_copy(t["yacc"][:], q2[:, 768:1024]), after=[b17, m2_])
                    tk_q2done[tt] = m3_
                    qprev = [m1_, m3_]
                else:
                    tk_q2done[tt] = b17
                    qprev = [b17]

        if is_l0:
            tk_bank_last = [None, None]
            for tt in range(TT):
                for dmt in range(NDM):
                    bnk = dmt % 2
                    reg = pbank[6 + bnk][:, ((dmt // 2) % 2) * 128:((dmt // 2) % 2) * 128 + 128]
                    tk_t = pe(lambda g, tt=tt, dmt=dmt, reg=reg:
                              g.transpose(reg, t["xn"][:, tt * c.DM + dmt * 128: tt * c.DM + (dmt + 1) * 128],
                                          t["idf"][:]),
                              after=[tk_xn[tt], tk_idf, tk_bank_last[bnk]] + f_b4567)
                    tk_e = act(lambda g, tt=tt, dmt=dmt, reg=reg:
                               g.copy(t["hin"][:, dmt * T + tt * 128: dmt * T + (tt + 1) * 128], reg), after=[tk_t])
                    tk_bank_last[bnk] = tk_e
                    tk_hin.append(tk_e)

        tk_u = [None] * NDT
        tk_gz = [None] * NDT
        bank_free = [None] * 6
        tk_xz_last = None
        conv_prev = []
        for e in range(NE):
            bk = (e + 2) % 6
            ps = pbank[bk][:, 0:T]
            mm = None
            for dmt in range(NDM):
                mm = pe(lambda g, e=e, dmt=dmt, ps=ps:
                        g.matmul(ps, t["wiT"][:, dmt * c.E + e * 128: dmt * c.E + (e + 1) * 128],
                                 t["hin"][:, dmt * T:(dmt + 1) * T],
                                 start=(dmt == 0), stop=(dmt == NDM - 1)),
                        after=(tk_hin + [bank_free[bk]] +
                               (f_b01 if bk < 2 else (f_b23 if bk < 4 else f_b4567))) if dmt == 0 else [])
                if dmt == NDM - 1:
                    tk_xz_last = mm
            if e < NDT:
                d = e
                c0 = dve(lambda g, d=d: g.tensor_copy(t["uraw"][:, 0:3], t["tails"][:, d * 3:(d + 1) * 3]),
                         after=[mm] + conv_prev + f_da + f_u)
                c1 = act(lambda g, ps=ps: g.copy(t["uraw"][:, 3:3 + T], ps), after=[mm, c0])
                bank_free[bk] = c1
                c2 = dve(lambda g, d=d: g.tensor_copy(t["tails"][:, d * 3:(d + 1) * 3], t["uraw"][:, T:T + 3]),
                         after=[c1])
                ca = t["cacc"].ap()[:, 0:T]
                c3 = dve(lambda g, d=d, ca=ca: g.tensor_scalar(ca, t["uraw"][:, 0:T],
                                                               t["convw"][:, d * 4:d * 4 + 1],
                                                               t["convb"][:, d:d + 1], OP.mult, OP.add), after=[c1])
                c4 = dve(lambda g, d=d, ca=ca:
                         g.scalar_tensor_tensor(ca, t["uraw"][:, 1:1 + T], t["convw"][:, d * 4 + 1:d * 4 + 2], ca,
                                                OP.mult, OP.add), after=[c3])
                c5 = dve(lambda g, d=d, ca=ca:
                         g.scalar_tensor_tensor(ca, t["uraw"][:, 2:2 + T], t["convw"][:, d * 4 + 2:d * 4 + 3], ca,
                                                OP.mult, OP.add), after=[c4])
                c6 = dve(lambda g, d=d, ca=ca:
                         g.scalar_tensor_tensor(ca, t["uraw"][:, 3:3 + T], t["convw"][:, d * 4 + 3:d * 4 + 4], ca,
                                                OP.mult, OP.add), after=[c5])
                c7 = act(lambda g, d=d, ca=ca: g.activation(t["u"][:, d * T:(d + 1) * T], ca, AF.Silu),
                         after=[c6] + f_u)
                conv_prev = [c2, c7]
                tk_u[d] = c7
            else:
                d = e - NDT
                c1 = act(lambda g, ps=ps, d=d: g.activation(t["gz"][:, d * T:(d + 1) * T], ps, AF.Silu),
                         after=[mm] + f_u)
                bank_free[bk] = c1
                tk_gz[d] = c1

        mmd = mmb = mmc = None
        for dt_i in range(NDT):
            mmd = pe(lambda g, dt_i=dt_i: g.matmul(pbank[7][0:64, 0:T],
                                                   t["wxdt"][:, dt_i * c.RL:(dt_i + 1) * c.RL],
                                                   t["u"][:, dt_i * T:(dt_i + 1) * T],
                                                   start=(dt_i == 0), stop=(dt_i == NDT - 1)),
                     after=(tk_u + f_b4567) if dt_i == 0 else [])
        for dt_i in range(NDT):
            mmb = pe(lambda g, dt_i=dt_i: g.matmul(pbank[6][0:16, 0:T],
                                                   t["wxB"][:, dt_i * NST:(dt_i + 1) * NST],
                                                   t["u"][:, dt_i * T:(dt_i + 1) * T],
                                                   start=(dt_i == 0), stop=(dt_i == NDT - 1)),
                     after=([tk_hin[-1]] if is_l0 else []) if dt_i == 0 else [])
        for dt_i in range(NDT):
            mmc = pe(lambda g, dt_i=dt_i: g.matmul(pbank[4][0:16, 0:T],
                                                   t["wxC"][:, dt_i * NST:(dt_i + 1) * NST],
                                                   t["u"][:, dt_i * T:(dt_i + 1) * T],
                                                   start=(dt_i == 0), stop=(dt_i == NDT - 1)),
                     after=[bank_free[4]] if dt_i == 0 else [])
        ev_dt = act(lambda g: g.copy(t["dtlow"][:], pbank[7][0:64, 0:T]), after=[mmd] + f_da)
        ev_b = act(lambda g: g.activation(t["sbBb"][:], pbank[6][0:16, 0:T], AF.Copy, scale=-1.0),
                   after=[mmb] + f_da)
        ev_c = act(lambda g: g.copy(t["sbCb"][:], pbank[4][0:16, 0:T]), after=[mmc] + f_da)
        tk_bt = [ev_b]
        btwar = ev_b
        tk_crep = []
        for n in range(NST):
            bk, half = n // 2, n % 2
            reg = pbank[bk][:, half * T: half * T + T]
            mmn = pe(lambda g, n=n, reg=reg: g.matmul(reg, t["dlt"][:, n * 128:(n + 1) * 128], t["sbCb"][:],
                                                      start=True, stop=True),
                     after=[ev_c, tk_dlt, tk_bt[-1] if tk_bt else None] +
                           ([bank_free[bk]] if bk < 6 else []) +
                           ([ev_dt] if bk == 7 else []) + ([ev_b] if bk == 6 else []))
            evn = act(lambda g, n=n, reg=reg: g.copy(t["crep"][:, n * T:(n + 1) * T], reg),
                      after=[mmn] + (f_crep if n == 0 else []))
            tk_crep.append(evn)

        tk_qxT = []
        if not is_l0:
            tk_cst = []
            tk_tpq = []
            for tt in range(TT):
                for dmt in range(NDM):
                    k = tt * NDM + dmt
                    qxsb = t["qxs"] if k % 2 == 0 else t["qxs2"]
                    regb = pbank[4 + (k % 2)].ap().bitcast(BF16)[:, 512:640]
                    if tt == 0:
                        if dmt < 4:
                            qsrc = t["hsc"].ap()[:, dmt * 128:(dmt + 1) * 128]
                        elif dmt < 6:
                            qsrc = t["vneg"].ap()[:, (dmt - 4) * 128:(dmt - 3) * 128]
                        else:
                            qsrc = t["yacc"].ap()[:, (dmt - 6) * 128:(dmt - 5) * 128]
                    else:
                        qsrc = t["q2"].ap()[:, dmt * 128:(dmt + 1) * 128]
                    cst = dve(lambda g, qsrc=qsrc, qxsb=qxsb:
                              g.tensor_scalar(qxsb[:], qsrc, 2.0 ** -9, None, OP.mult),
                              after=[tk_q2done[tt], (tk_tpq[-2] if k >= 2 else None)] +
                                    (f_hin if k < 2 else []))
                    tpq = pe(lambda g, regb=regb, qxsb=qxsb: g.transpose(regb, qxsb[:], t["idb"][:]),
                             after=[cst, tk_idb, (tk_qxT[-2] if k >= 2 else None)] +
                                   ([tk_crep[-1]] + f_o1 if k < 2 else []))
                    evq = act(lambda g, dmt=dmt, tt=tt, regb=regb:
                              g.copy(t["qxT"][:, dmt * T + tt * 128: dmt * T + (tt + 1) * 128], regb),
                              after=[tpq])
                    tk_cst.append(cst)
                    tk_tpq.append(tpq)
                    tk_qxT.append(evq)

        tk_pe_phaseA = pl.tok("tensor")

        mm4_first = set()
        tk_dt_war = []
        _q = tk_qxT[-1] if tk_qxT else None
        tk_dA_war = [_q, _q, _q, _q]
        tk_h_war = [[_q] if _q else None, [_q] if _q else None]
        tk_mdt_readers = [[], []]
        tk_sp_war = None
        tk_brep_war = [[], []]
        tk_dbu_war = [None, None]
        tk_vneg_war = None
        tk_hC_war = [None, None]
        tk_yaccA_war = None
        tk_yg_war = None
        tk_mm4_last = None
        sg3p = [None, None]

        def emit_pchain(dt_i, extra):
            mmq = pe(lambda g, dt_i=dt_i: g.matmul(pbank[4].ap()[:, 0:T], t["wdtT"][:, dt_i * 128:(dt_i + 1) * 128],
                                                   t["dtlow"][:], start=True, stop=True),
                     after=[ev_dt] + extra)
            p1 = act(lambda g, dt_i=dt_i: g.activation(t["aa"][:], pbank[4].ap()[:, 0:T], AF.Abs,
                                                       bias=t["pbdt"][:, dt_i:dt_i + 1]),
                     after=[mmq, tk_sp_war])
            p2 = act(lambda g: g.activation(t["aa"][:], t["aa"][:], AF.Exp, scale=-1.0), after=[p1])
            p3 = act(lambda g: g.activation(t["ll"][:], t["aa"][:], AF.Ln, bias=t["stat"][:, 6:7]), after=[p2, tk_cb2])
            p4 = act(lambda g, dt_i=dt_i: g.activation(t["rr"][:], pbank[4].ap()[:, 0:T], AF.Relu,
                                                       bias=t["pbdt"][:, dt_i:dt_i + 1]), after=[p3])
            return p4

        p4_next = emit_pchain(0, [tk_pe_phaseA, tk_crep[-1], ev_b] + tk_qxT[-1:])

        def dA_slot0(n):
            q = n % 4
            return (t["dA"].ap()[:, 0:T], t["dA"].ap()[:, T:2 * T],
                    t["dA2"][:, 0:T], t["dA2"][:, T:2 * T])[q]

        def emit_head(dt_n, p4_n, first=False):
            par_n = dt_n % 2
            mdtb_n = t["mdtB"] if par_n == 0 else t["mdtB1"]
            sg3_n = dve(lambda g, mdtb_n=mdtb_n: g.scalar_tensor_tensor(mdtb_n[:], t["ll"][:], -1.0, t["rr"][:],
                                                                        OP.mult, OP.subtract),
                        after=[p4_n] + tk_mdt_readers[par_n] + ([tk_state0] if (s == 0 and first) else []))
            tk_mdt_readers[par_n] = []
            exp0 = []
            for j2 in range(2):
                exk = act(lambda g, dt_n=dt_n, j2=j2, mdtb_n=mdtb_n:
                          g.activation(dA_slot0(j2), mdtb_n[:], AF.Exp,
                                       scale=t["negA"][:, dt_n * NST + j2: dt_n * NST + j2 + 1]),
                          after=[sg3_n, tk_dA_war[j2]])
                tk_mdt_readers[par_n].append(exk)
                exp0.append(exk)
            brep0 = []
            for j in range(2):
                brep0.append(pe(lambda g, j=j: g.matmul(pbank[6].ap()[:, j * T:(j + 1) * T],
                                                        t["dlt"][:, j * 128:(j + 1) * 128], t["sbBb"][:],
                                                        start=True, stop=True),
                                after=[ev_b, tk_dlt] + tk_brep_war[0] +
                                      ([tk_pe_phaseA] if first else [])))
            return sg3_n, exp0, brep0

        head_cur = emit_head(0, p4_next, first=True)
        for dt_i in range(NDT):
            par = dt_i % 2
            mdtb = t["mdtB"] if par == 0 else t["mdtB1"]
            p4 = p4_next
            tk_dt_war = [p4]
            sg3, exp0_cur, brep0_cur = head_cur
            tk_sp_war = sg3
            sg3p[par] = sg3
            v1t = dve(lambda g, dt_i=dt_i, mdtb=mdtb: g.tensor_tensor(t["vneg"][:], mdtb[:],
                                                                      t["u"][:, dt_i * T:(dt_i + 1) * T], OP.mult),
                      after=[sg3, tk_u[dt_i], tk_vneg_war])
            tk_mdt_readers[par].append(v1t)

            def pair_slot(gi):
                return t["hsc"].ap()[:, 0:2 * T] if gi % 2 == 0 else t["hscB"]

            def dA_slot(n):
                q = n % 4
                return (t["dA"].ap()[:, 0:T], t["dA"].ap()[:, T:2 * T],
                        t["dA2"][:, 0:T], t["dA2"][:, T:2 * T])[q]

            def emit_exp(gi2):
                toks = []
                for j2 in range(2):
                    n2 = gi2 * 2 + j2
                    dAb2 = dA_slot(n2)
                    exk = act(lambda g, dt_i=dt_i, n2=n2, dAb2=dAb2, mdtb=mdtb:
                              g.activation(dAb2, mdtb[:], AF.Exp,
                                           scale=t["negA"][:, dt_i * NST + n2: dt_i * NST + n2 + 1]),
                              after=[sg3, tk_dA_war[n2 % 4]])
                    tk_mdt_readers[par].append(exk)
                    toks.append(exk)
                return toks

            tk_exp_pair = exp0_cur
            for gi in range(NST // 2):
                bb = 6 + (gi % 2)
                if gi == 0:
                    mmbr = brep0_cur
                else:
                    mmbr = []
                    for j in range(2):
                        n = gi * 2 + j
                        mmbr.append(pe(lambda g, n=n, j=j, bb=bb:
                                       g.matmul(pbank[bb].ap()[:, j * T:(j + 1) * T],
                                                t["dlt"][:, n * 128:(n + 1) * 128], t["sbBb"][:],
                                                start=True, stop=True),
                                       after=[ev_b, tk_dlt] + tk_brep_war[gi % 2] +
                                             ([tk_pe_phaseA] if dt_i == 0 and gi < 2 else [])))
                dbus = []
                for j in range(2):
                    dbuj = dve(lambda g, bb=bb, j=j:
                               g.tensor_tensor(pbank[5].ap()[:, j * T:(j + 1) * T],
                                               t["vneg"][:], pbank[bb].ap()[:, j * T:(j + 1) * T],
                                               OP.mult),
                               after=[mmbr[j], v1t, tk_dbu_war[j]] +
                                     ([tk_qxT[-1]] if (dt_i == 0 and gi == 0 and tk_qxT) else []))
                    dbus.append(dbuj)
                tk_brep_war[gi % 2] = dbus
                ps_ = pair_slot(gi)
                scs = []
                for j in range(2):
                    n = gi * 2 + j
                    hb = ps_[:, j * T:(j + 1) * T]
                    sc = dve(lambda g, hb=hb, dt_i=dt_i, n=n, j=j:
                             g.tensor_tensor_scan(hb, dA_slot(n), pbank[5].ap()[:, j * T:(j + 1) * T],
                                                  t["state"][:, dt_i * NST + n: dt_i * NST + n + 1],
                                                  OP.mult, OP.add),
                             after=[tk_exp_pair[j], dbus[j]] + (tk_h_war[gi % 2] or []))
                    tk_dA_war[n % 4] = sc
                    scs.append(sc)
                tk_dbu_war = scs
                if gi + 1 < NST // 2:
                    tk_exp_pair = emit_exp(gi + 1)
                stcs = []
                for j in range(2):
                    n = gi * 2 + j
                    hb = ps_[:, j * T:(j + 1) * T]
                    stc = act(lambda g, hb=hb, dt_i=dt_i, n=n:
                              g.copy(t["state"][:, dt_i * NST + n: dt_i * NST + n + 1], hb[:, T - 1:T]),
                              after=[scs[j]])
                    stcs.append(stc)
                n0 = gi * 2
                slot = gi % 2
                hcp0 = dve(lambda g, n0=n0, slot=slot, ps_=ps_: g.tensor_tensor(
                               t["gsml"].ap()[:, slot * 2 * T: slot * 2 * T + T], ps_[:, 0:T],
                               t["crep"][:, n0 * T:(n0 + 1) * T], OP.mult),
                           after=[scs[0], tk_crep[n0]] + (tk_hC_war[slot] or []))
                hcp1 = gp(lambda g, n0=n0, slot=slot, ps_=ps_: g.tensor_tensor(
                              t["gsml"].ap()[:, slot * 2 * T + T: slot * 2 * T + 2 * T], ps_[:, T:2 * T],
                              t["crep"][:, (n0 + 1) * T:(n0 + 2) * T], OP.mult),
                          after=[scs[1], tk_crep[n0 + 1]] + (tk_hC_war[slot] or []))
                tk_h_war[slot] = [hcp0, hcp1] + stcs
                if gi == 0:
                    a1_ = gp(lambda g, slot=slot: g.tensor_copy(t["yacc"][:], t["gsml"].ap()[:, slot * 2 * T: slot * 2 * T + T]),
                             after=[hcp0, tk_yaccA_war])
                else:
                    a1_ = gp(lambda g, slot=slot: g.tensor_tensor(t["yacc"][:], t["yacc"][:],
                                                                  t["gsml"].ap()[:, slot * 2 * T: slot * 2 * T + T], OP.add),
                             after=[hcp0, tk_yaccA_war])
                a2_ = gp(lambda g, slot=slot: g.tensor_tensor(t["yacc"][:], t["yacc"][:],
                                                              t["gsml"].ap()[:, slot * 2 * T + T: slot * 2 * T + 2 * T], OP.add),
                         after=[a1_, hcp1])
                tk_yaccA_war = a2_
                tk_hC_war[slot] = [a2_]
                if gi == 2 and dt_i + 1 < NDT:
                    p4_next = emit_pchain(dt_i + 1, tk_dt_war)

            if dt_i + 1 < NDT:
                head_cur = emit_head(dt_i + 1, p4_next)
            du = dve(lambda g, dt_i=dt_i: g.scalar_tensor_tensor(t["hsc"][:, 0:T], t["u"][:, dt_i * T:(dt_i + 1) * T],
                                                                 t["dvec"][:, dt_i:dt_i + 1], t["yacc"][:],
                                                                 OP.mult, OP.add),
                     after=[tk_yaccA_war] + (tk_h_war[0] or []) + (tk_h_war[1] or []))
            tk_yaccA_war = du
            yg = dve(lambda g, dt_i=dt_i: g.tensor_tensor(t["yg"][:], t["hsc"][:, 0:T],
                                                          t["gz"][:, dt_i * T:(dt_i + 1) * T],
                                                          OP.mult), after=[du, tk_gz[dt_i], tk_yg_war])
            tk_h_war[0] = [yg]
            tk_hC_war[0] = (tk_hC_war[0] or []) + [yg]
            tk_hC_war[1] = (tk_hC_war[1] or []) + [yg]
            last_mm = None
            for ot in range(NDM):
                bk = ot // 2
                reg = pbank[bk][:, (ot % 2) * T:(ot % 2) * T + T]
                fb_ = bk not in mm4_first
                if fb_:
                    mm4_first.add(bk)
                last_mm = pe(lambda g, dt_i=dt_i, ot=ot, reg=reg, fb_=fb_:
                             g.matmul(reg, t["woT"][:, dt_i * c.DM + ot * 128: dt_i * c.DM + (ot + 1) * 128],
                                      t["yg"][:], start=fb_, stop=(dt_i == NDT - 1 and ot % 2 == 1)),
                             after=[yg] + ([tk_pe_phaseA, tk_crep[-1]] if dt_i == 0 and ot == 0 else []))
            tk_yg_war = last_mm
            tk_mm4_last = last_mm
        tk_dbu_war_final = list(tk_dbu_war)
        tk_dtbank_war = [tk_dt_war, []]

        sub_end = []
        if is_l0:
            prev_e = [None, None]
            tk_l0ev = []
            for ot in range(NDM):
                bk = ot // 2
                reg = pbank[bk][:, (ot % 2) * T:(ot % 2) * T + T]
                dst = t["h1st"] if ot % 2 == 0 else t["h1st2"]
                ev = act(lambda g, reg=reg, dst=dst: g.copy(dst[:], reg), after=[tk_mm4_last, prev_e[ot % 2]])
                tk_l0ev.append(ev)
                dmo = dma(dr["h1s"][ot * 128:(ot + 1) * 128, xoff:xoff + T], dst[:], after=[ev], ch="ch1")
                prev_e[ot % 2] = dmo
                sub_end.append(dmo)
        else:
            tk_tp = []
            tk_tp_bank = {}
            prev_e = [None, None]
            tk_b01_free = []
            for ot in range(NDM):
                bk = ot // 2
                reg = pbank[bk][:, (ot % 2) * T:(ot % 2) * T + T]
                h2b = t["h2"] if ot % 2 == 0 else t["mdtB"]
                ev = act(lambda g, reg=reg, h2b=h2b: g.copy(h2b[:], reg),
                         after=[tk_mm4_last, prev_e[ot % 2]] + tk_dt_war +
                               (tk_mdt_readers[0] + tk_mdt_readers[1] if ot < 2 else []))
                if ot < 4:
                    tk_b01_free.append(ev)
                for tt in range(TT):
                    tb = 4 + tt * 2 + (ot // 4)
                    col = (ot % 4) * 128
                    tp = pe(lambda g, tb=tb, col=col, tt=tt, h2b=h2b:
                            g.transpose(pbank[tb][:, col:col + 128], h2b[:, tt * 128:(tt + 1) * 128],
                                        t["idf"][:]),
                            after=[ev] + tk_dbu_war_final + tk_brep_war)
                    tk_tp.append(tp)
                    tk_tp_bank.setdefault(tb, []).append(tp)
                prev_e[ot % 2] = tk_tp[-1]
            tk_ev23 = [ev]
            tk_fs_last = []
            tk_o1s = []
            tk_o3s = []
            ep_prev = None
            gs_last = None
            for tt in range(TT):
                gtoks = []
                for half in range(2):
                    mmf = None
                    for dmt in range(NDM):
                        mmf = pe(lambda g, tt=tt, half=half, dmt=dmt:
                                 g.matmul(pbank[half][:, 0:512],
                                          t["qxT"][:, dmt * T + tt * 128: dmt * T + (tt + 1) * 128],
                                          t["w8bT"][:, dmt * c.DM + half * 512: dmt * c.DM + (half + 1) * 512],
                                          start=(dmt == 0), stop=(dmt == NDM - 1)),
                                 after=(tk_b01_free + [gs_last, ep_prev] + tk_qxT) if dmt == 0 else [])
                    fs = dve(lambda g, half=half, tt=tt:
                             g.scalar_tensor_tensor(t["fsc"][:], pbank[half][:, 0:512], t["wsxs"][:, tt:tt + 1],
                                                    t["brep"][:, half * 512:(half + 1) * 512], OP.mult, OP.add),
                             after=[mmf, tk_wsxs[tt], gs_last])
                    gs_ = act(lambda g, half=half:
                              g.activation(t["gsml"][:, half * 512:(half + 1) * 512],
                                           t["fsc"][:], AF.Silu), after=[fs])
                    gtoks.append(gs_)
                    gs_last = gs_
                    tk_fs_last = [fs]
                tk_xr = dma(t["x"][:, 0:c.DM], dr["x"][xoff + tt * 128: xoff + (tt + 1) * 128, :],
                            after=[tk_xn[1], ep_prev, tk_dA_war[2], tk_dA_war[3]] +
                                  (tk_h_war[1] or []), ch="cx")
                otoks = []
                for piece in range(2):
                    bkp = 4 + tt * 2 + piece
                    o1 = dve(lambda g, tt=tt, piece=piece, bkp=bkp:
                             g.tensor_tensor(t["ep1"][:, piece * 512:(piece + 1) * 512], pbank[bkp][:, 0:512],
                                             t["xn"][:, tt * c.DM + piece * 512: tt * c.DM + (piece + 1) * 512],
                                             OP.add),
                             after=tk_tp_bank[bkp] + [tk_xn[tt], ep_prev] + gtoks)
                    o2 = dve(lambda g, piece=piece:
                             g.tensor_tensor(t["ep1"][:, piece * 512:(piece + 1) * 512],
                                             t["ep1"][:, piece * 512:(piece + 1) * 512],
                                             t["gsml"][:, piece * 512:(piece + 1) * 512],
                                             OP.mult), after=[o1] + gtoks)
                    o3 = dve(lambda g, piece=piece:
                             g.tensor_tensor(t["ep1"][:, piece * 512:(piece + 1) * 512],
                                             t["ep1"][:, piece * 512:(piece + 1) * 512],
                                             t["x"][:, piece * 512:(piece + 1) * 512],
                                             OP.add), after=[o2, tk_xr])
                    otoks.append(o3)
                    tk_o1s.append(o1)
                    tk_o3s.append(o3)
                dmo = dma(dr["out"][xoff + tt * 128: xoff + (tt + 1) * 128, :], t["ep1"][:, 0:c.DM], after=otoks, ch="cout")
                ep_prev = dmo
                sub_end.append(dmo)
        nfine = {"full": sub_end}
        if is_l0:
            nfine["xfree"] = [tk_xn[-1]]
            nfine["o1s"] = []
            nfine["hinfree"] = [tk_xz_last]
            nfine["b01"] = [tk_l0ev[3]]
            nfine["b23"] = [tk_l0ev[7]]
            nfine["b4567"] = tk_dbu_war_final + tk_brep_war + tk_dt_war
        else:
            nfine["xfree"] = [tk_o3s[-1]]
            nfine["o1s"] = tk_o1s
            nfine["hinfree"] = [tk_xz_last]
            nfine["b01"] = tk_fs_last
            nfine["b23"] = tk_ev23
            nfine["b4567"] = tk_o1s[-1:]
            nfine["qfree"] = [sub_end[-1]]
        nfine["crepfree"] = tk_hC_war[0] + tk_hC_war[1]
        nfine["dafree"] = [tk_dA_war[0], tk_dA_war[1], tk_dA_war[2], tk_dA_war[3]]
        nfine["ufree"] = [tk_yaccA_war, tk_yg_war]
        return nfine

    w0 = plan_layer_weights(0, after=[tk_flag])
    fine = {"full": w0 + [tk_state0, tk_tails0, tk_dlt, tk_idb]}
    for s in range(c.NSUB):
        fine = plan_subtile(0, s, fine)
    prev = fine["full"] + fine["dafree"] + fine["ufree"] + fine["crepfree"] + fine["b4567"] + fine["hinfree"]
    tk_s1 = dve(lambda g: g.memset(t["state"][:], 0.0), after=prev)
    tk_t1 = dve(lambda g: g.memset(t["tails"][:], 0.0), after=prev)
    w1 = plan_layer_weights(1, after=prev + [tk_s1, tk_t1])
    wf = plan_ffn_weights(after=prev + [w1[-1]])
    fine = {"full": w1 + wf + [tk_s1, tk_t1]}
    for s in range(c.NSUB):
        fine = plan_subtile(1, s, fine)

    finals = [(ch, pl.cnt[ch]) for ch in DMACH if pl.cnt[ch] > 0]
    pl.finalize()

    with nc.Block() as block:
        @block.sync
        def _(sync):
            pl.replay_all("sync", sync, sems)
            for ch, v in finals:
                sync.wait_ge(sems[ch], v)

        @block.tensor
        def _(tensor):
            pl.replay_all("tensor", tensor, sems)

        @block.vector
        def _(vector):
            pl.replay_all("vector", vector, sems)

        @block.scalar
        def _(scalar):
            pl.replay_all("scalar", scalar, sems)

        @block.gpsimd
        def _(gpsimd):
            pl.replay_all("gpsimd", gpsimd, sems)

    S.close()
    return nc, dr


def prep_inputs(core_x, ins, cfg):
    c = cfg
    d = {"x": np.ascontiguousarray(core_x)}
    for l in range(2):
        d[f"wiT{l}"] = np.ascontiguousarray(ins["L0_attn_W_in"][l].T)
        d[f"woT{l}"] = np.ascontiguousarray(ins["L0_attn_W_out"][l].T)
        wx = ins["L0_attn_W_x"][l]
        d[f"wxdt{l}"] = np.ascontiguousarray(wx[:c.RL].T)
        d[f"wxB{l}"] = np.ascontiguousarray(wx[c.RL:c.RL + c.NST].T)
        d[f"wxC{l}"] = np.ascontiguousarray(wx[c.RL + c.NST:].T)
        d[f"wdtT{l}"] = np.ascontiguousarray(ins["L0_attn_W_dt"][l].T)
        d[f"alog{l}"] = np.ascontiguousarray(
            ins["L0_attn_A_log"][l].reshape(c.NDT, 128, c.NST).transpose(1, 0, 2).reshape(128, c.NDT * c.NST))
        d[f"convw{l}"] = np.ascontiguousarray(
            ins["L0_attn_conv_w"][l].reshape(c.NDT, 128, 4).transpose(1, 0, 2).reshape(128, c.NDT * 4))
        d[f"convb{l}"] = np.ascontiguousarray(ins["L0_attn_conv_b"][l].reshape(c.NDT, 128).T)
        d[f"dvec{l}"] = np.ascontiguousarray(ins["L0_attn_D"][l].reshape(c.NDT, 128).T)
        d[f"bdt{l}"] = np.ascontiguousarray(ins["L0_attn_b_dt"][l].reshape(c.NDT, 128).T)
    d["w8T"] = np.ascontiguousarray(np.asarray(ins["L0_ffn_weight"]).view(np.uint8).T)
    d["fbias"] = np.ascontiguousarray(np.asarray(ins["L0_ffn_bias"], np.float32).reshape(1, -1))
    d["wscale"] = np.asarray(ins["L0_ffn_weight_scale"], np.float32).reshape(1, 1)
    return d




_CFG = Cfg(L=4096, T=256)
_NC_CACHE = {}


def _build_program():
    if "nc" not in _NC_CACHE:
        nc = bass.Bass("TRN2", target_bir_lowering=False)
        nc, dr = build(_CFG, nc)
        _NC_CACHE["nc"] = nc
    return _NC_CACHE["nc"]


def kernel(L0_attn_x, L0_attn_W_in, L0_attn_conv_w, L0_attn_conv_b, L0_attn_W_x,
           L0_attn_W_dt, L0_attn_b_dt, L0_attn_A_log, L0_attn_D, L0_attn_W_out,
           L0_ffn_weight, L0_ffn_weight_scale, L0_ffn_bias):
    from concourse.bass_utils import run_bass_kernel_spmd
    ins = dict(L0_attn_W_in=np.asarray(L0_attn_W_in, np.float32),
               L0_attn_conv_w=np.asarray(L0_attn_conv_w, np.float32),
               L0_attn_conv_b=np.asarray(L0_attn_conv_b, np.float32),
               L0_attn_W_x=np.asarray(L0_attn_W_x, np.float32),
               L0_attn_W_dt=np.asarray(L0_attn_W_dt, np.float32),
               L0_attn_b_dt=np.asarray(L0_attn_b_dt, np.float32),
               L0_attn_A_log=np.asarray(L0_attn_A_log, np.float32),
               L0_attn_D=np.asarray(L0_attn_D, np.float32),
               L0_attn_W_out=np.asarray(L0_attn_W_out, np.float32),
               L0_ffn_weight=L0_ffn_weight,
               L0_ffn_weight_scale=L0_ffn_weight_scale,
               L0_ffn_bias=np.asarray(L0_ffn_bias, np.float32))
    x = np.asarray(L0_attn_x, np.float32)
    nc = _build_program()
    wshared = prep_inputs(x[0], ins, _CFG)
    in_maps = []
    for b in range(8):
        d = dict(wshared)
        d["x"] = np.ascontiguousarray(x[b])
        in_maps.append(d)
    res = run_bass_kernel_spmd(nc, in_maps, core_ids=list(range(8)))
    out = np.stack([np.asarray(r["out"], np.float32) for r in res.results], axis=0)
    return out
